# version 16 (final)
# baseline (speedup 1.0000x reference)
import math
import numpy as np
import concourse.bass as bass
import concourse.mybir as mybir
from concourse.bass_utils import run_bass_kernel_spmd

F32 = mybir.dt.float32
BF16 = mybir.dt.bfloat16
F16 = mybir.dt.float16
AF = mybir.ActivationFunctionType
ALU = mybir.AluOpType
AX = mybir.AxisListType


class Cfg:
    def __init__(self, T=2048, D=1024, H=8, NSEQ=2, TT=256, IT=512):
        self.T, self.D, self.H, self.NSEQ = T, D, H, NSEQ
        self.DH = 128
        self.CH = D // 128
        self.TT = min(TT, T)
        self.IT = min(IT, T)
        self.JT = 128
        self.NV = self.IT // self.JT
        self.nTT = T // self.TT
        self.nI = T // self.IT
        self.nJ = T // self.JT
        self.HB = T // 2
        self.ST = min(512, self.HB)
        self.nST = self.HB // self.ST
        self.HG = max(1, H // 2)
        assert D == H * self.DH and self.HB % self.ST == 0
        assert self.IT % self.TT == 0


class Prog:
    ENGINES = ("pe", "dve", "act", "gp", "sp")

    def __init__(self, nc):
        self.nc = nc
        self.q = {e: [] for e in self.ENGINES}
        self.sems = {}
        self.cnt = {}
        self.lastw = {}
        self.last_op = {}

    def sem(self, name):
        if name not in self.sems:
            self.sems[name] = self.nc.semaphore(name).__enter__()
            self.cnt[name] = 0
        return self.sems[name]

    def op(self, eng, fn, inc=None, n=1):
        if inc is None:
            self.q[eng].append(fn)
            self.last_op[eng] = len(self.q[eng]) - 1
            return None
        h = self.sem(inc)
        self.cnt[inc] += n
        self.q[eng].append(lambda e, fn=fn, h=h, n=n: fn(e).then_inc(h, n))
        self.last_op[eng] = None
        return self.cnt[inc]

    def tick(self, eng, sem, n=1):
        h = self.sem(sem)
        self.cnt[sem] += n
        li = self.last_op.get(eng)
        if li is not None and li == len(self.q[eng]) - 1:
            fn = self.q[eng][li]
            self.q[eng][li] = lambda e, fn=fn, h=h, n=n: fn(e).then_inc(h, n)
            self.last_op[eng] = None
        else:
            self.q[eng].append(lambda e, h=h, n=n: e.sem_inc(h, n))
        return self.cnt[sem]

    def wait(self, eng, sem, val):
        if not val:
            return
        assert val <= self.cnt.get(sem, 0), \
            f"wait on future {sem}={val} (have {self.cnt.get(sem)})"
        key = (eng, sem)
        if self.lastw.get(key, 0) >= val:
            return
        self.lastw[key] = val
        h = self.sem(sem)
        self.q[eng].append(lambda e, h=h, v=val: e.wait_ge(h, v))

    def now(self, sem):
        self.sem(sem)
        return self.cnt[sem]

    def fop(self, eng, fn):
        self.op(eng, fn)
        self.fence(eng, "s_dve" if eng == "dve" else "s_act")

    def barrier(self):
        for e in ("pe", "dve", "act"):
            self.tick(e, "s_bar")
        v = self.now("s_bar")
        for e in ("pe", "dve", "act", "gp", "sp"):
            self.wait(e, "s_bar", v)

    def fence(self, eng, sem):
        self.wait(eng, sem, self.tick(eng, sem))


def build_nc(c: Cfg, stage="full"):
    import os
    AB = set(os.environ.get("KAB", "").split(","))
    nc = bass.Bass("TRN2", target_bir_lowering=False, detect_race_conditions=False)
    CH, T, H, DH, D = c.CH, c.T, c.H, c.DH, c.D
    A = nc.alloc_sbuf_tensor
    AT = nc.alloc_sbuf_tensor_at

    x_hbm = nc.dram_tensor("x", [c.NSEQ, D, T], F32, kind="ExternalInput").ap()
    wq_h = nc.dram_tensor("wq", [H, D, DH], F32, kind="ExternalInput").ap()
    wk_h = nc.dram_tensor("wk", [H, D, DH], F32, kind="ExternalInput").ap()
    wv_h = nc.dram_tensor("wv", [H, D, DH], F32, kind="ExternalInput").ap()
    wo_h = nc.dram_tensor("wo", [H, DH, D], F32, kind="ExternalInput").ap()
    wrT_h = nc.dram_tensor("wrT", [D, D], F32, kind="ExternalInput").ap()
    wkT_h = nc.dram_tensor("wkT", [D, D], F32, kind="ExternalInput").ap()
    wvT_h = nc.dram_tensor("wvT", [D, D], F32, kind="ExternalInput").ap()
    mur_h = nc.dram_tensor("mur", [D], F32, kind="ExternalInput").ap()
    muk_h = nc.dram_tensor("muk", [D], F32, kind="ExternalInput").ap()
    out_hbm = nc.dram_tensor("out", [c.NSEQ, D, T], F32, kind="ExternalOutput").ap()
    dbg_hbm = nc.dram_tensor("dbg", [128, 64], F32, kind="ExternalOutput").ap() \
        if any(k in AB for k in ("dbg", "dbgx", "dbge", "dbgs")) else None

    K_sb = A("K_sb", [128, H, T], BF16)
    V_sb = A("V_sb", [128, c.nJ, H * DH], BF16)
    big_bytes = 2 * H * T * 2
    big = nc.sbuf_base - big_bytes
    assert 3 * CH * c.HB * 2 <= big_bytes and 2 * T * 4 <= big_bytes
    xr_sb = AT("xr_sb", [128, CH, c.HB], BF16, offset=big)
    xk_sb = AT("xk_sb", [128, CH, c.HB], BF16, offset=big + CH * c.HB * 2)
    r_sb = AT("r_sb", [128, CH, c.HB], BF16, offset=big + 2 * CH * c.HB * 2)
    ka_sb = AT("ka_sb", [128, CH, c.HB], BF16, offset=big)
    tabC = AT("tabC", [128, T], F32, offset=big)
    tabS = AT("tabS", [128, T], F32, offset=big + T * 4)
    assert 3 * CH * c.HB * 2 + 4 * c.TT * 2 <= big_bytes
    stash = AT("stash", [128, 4 * c.TT], F16, offset=big + 3 * CH * c.HB * 2)

    x_sb = A("x_sb", [128, CH, T], F32)
    wsl = [A(f"wsl{i}", [128, CH, CH, 128], BF16) for i in range(2)]
    wsl1_base = nc.sbuf_base - CH * CH * 128 * 2
    wvm_sb = AT("wvm", [128, CH, H * DH], BF16, offset=wsl1_base)

    SB = c.TT * CH * 2
    regS = A("regS", [128, CH * c.TT * 2], BF16)
    s_base = nc.sbuf_base - 2 * SB
    xbf_t = AT("xbf", [128, CH, c.TT], BF16, offset=s_base)
    xsq_t = AT("xsq", [128, CH, c.TT], BF16, offset=s_base + SB)
    scr_cols = max(T, c.HG * c.IT, c.TT * CH)
    if SB // 2 >= scr_cols:
        scr = AT("scr", [128, SB // 2], BF16, offset=s_base + SB)
    else:
        scr = A("scr", [128, scr_cols], BF16)
    O_sb = AT("O_sb", [128, H, c.IT], BF16, offset=s_base) \
        if H * c.IT * 2 <= 2 * SB else A("O_sb", [128, H, c.IT], BF16)

    xn_t = A("xn", [128, CH, c.TT], BF16)
    xn_base = nc.sbuf_base - CH * c.TT * 2
    xn2_t = A("xn2", [128, CH, c.TT], BF16)
    q_lo = A("q_lo", [128, c.HG, c.IT], BF16)
    assert (H - c.HG) * c.IT * 2 <= CH * c.TT * 2
    q_hi = AT("q_hi", [128, H - c.HG, c.IT], BF16, offset=xn_base)
    ncar = A("ncar", [128, CH, 1], BF16)
    pbuf = [A(f"pbuf{v}", [128, c.IT], BF16) for v in range(c.NV)]
    pbuf2 = [A(f"pbuf2_{i}", [128, c.IT], BF16) for i in range(2)]
    invl = A("invl", [128, c.IT], F32)
    mgk = AT("mgk", [128, c.IT], mybir.dt.int32, offset=nc.sbuf_base - c.IT * 4)

    bsrc_a = A("bsrc_a", [128, c.IT], F16)
    bsrc_b = A("bsrc_b", [128, c.IT], F16)
    a128 = A("a128", [128, c.IT], F16)
    b128 = A("b128", [128, c.IT], F16)
    rowbuf = A("rowbuf", [128, T], F16)
    rsc = A("rsc", [128, 2 * c.TT], F32)
    bq_r = A("bq_r", [128, 2 * H + 4], F32)
    negB = A("negB", [128, H + 1], F32)

    cosT = A("cosT", [128, T], F16)
    sinT = A("sinT", [128, T], F16)
    ones_bf = A("ones_bf", [128, 128], BF16)
    ones_f16 = A("ones_f16", [128, 128], F16)
    sel0 = A("sel0", [128, 128], F16)
    sel32 = A("sel32", [128, 128], F16)
    onec_bf = A("onec_bf", [128, 1], BF16)
    tri01 = A("tri01", [128, c.JT], BF16)
    mu_sb = A("mu_sb", [128, 4 * CH], F32)
    iot = A("iot", [128, 1], F32)
    tsc = A("tsc", [128, 8], F32)

    pb = [nc.alloc_psum_tensor(f"pb{i}", [128, 512], F32) for i in range(6)]
    pbW = nc.alloc_psum_tensor("pbW", [128, 1024], F32)

    def q_ap(h):
        return q_lo[:, h, :] if h < c.HG else q_hi[:, h - c.HG, :]

    p = Prog(nc)
    LOG1E4_64 = math.log(10000.0) / 64.0
    c_hi = float(np.float32(LOG1E4_64))
    c_lo = float(np.float32(np.float64(LOG1E4_64) - np.float64(np.float32(LOG1E4_64))))
    EPS = 1e-5
    NLEV = int(round(math.log2(T)))
    assert (1 << NLEV) == T

    bank_guard = {}

    def pe_acquire(bi):
        g = bank_guard.get(bi)
        if g:
            p.wait("pe", g[0], g[1])

    def bank_release(bi, sem, cnt):
        bank_guard[bi] = (sem, cnt)

    def dma(q, out_ap, in_ap, sem):
        return p.op(q, lambda e, o=out_ap, i=in_ap: e.dma_start(out=o, in_=i),
                    inc=sem, n=16)

    slot_guard = {0: None, 1: None}

    def load_slot(slot, out_ap, in_ap):
        g = slot_guard[slot]
        if g:
            p.wait("gp", g[0], g[1])
        return dma("gp", out_ap, in_ap, f"s_w{slot}")

    W_FFN = {"r": wrT_h, "k": wkT_h, "v": wvT_h}

    def load_ffn(kind, slot):
        cnt = load_slot(slot, wsl[slot][:, :, :, :],
                        W_FFN[kind].rearrange("(ck p) (dk d) -> p ck dk d", p=128, d=128))
        if kind == "v":
            p.wait("dve", f"s_w{slot}", cnt)
            p.op("dve", lambda e, slot=slot: e.tensor_scalar_mul(
                out=wsl[slot].rearrange("p a b d -> p (a b d)"),
                in0=wsl[slot].rearrange("p a b d -> p (a b d)"), scalar1=0.5))
            return ("s_dve", p.tick("dve", "s_dve"))
        return (f"s_w{slot}", cnt)

    def load_attn(kind, slot):
        if kind == "wo":
            cnt = load_slot(slot, wsl[slot][:, :, :, :],
                            wo_h.rearrange("h p (mk m) -> p h mk m", m=128))
            return (f"s_w{slot}", cnt)
        if kind == "wvm":
            g = slot_guard[slot]
            if g:
                p.wait("gp", g[0], g[1])
            for h in range(H):
                cnt = dma("gp", wvm_sb[:, :, h * DH:(h + 1) * DH],
                          wv_h[h].rearrange("(ck p) d -> p ck d", p=128), f"s_w{slot}")
            return (f"s_w{slot}", cnt)
        src = {"wq": wq_h, "wk": wk_h}[kind]
        cnt = load_slot(slot, wsl[slot][:, :, :, :],
                        src.rearrange("h (ck p) d -> p h ck d", p=128))
        sc_ = float(128.0 ** -0.5)
        p.wait("dve", f"s_w{slot}", cnt)
        p.op("dve", lambda e, slot=slot, sc_=sc_: e.tensor_scalar_mul(
            out=wsl[slot].rearrange("p a b d -> p (a b d)"),
            in0=wsl[slot].rearrange("p a b d -> p (a b d)"),
            scalar1=sc_))
        return ("s_dve", p.tick("dve", "s_dve"))

    p.op("gp", lambda e: e.iota(iot[:, 0:1], [[1, 1]], channel_multiplier=1,
                                allow_small_or_imprecise_dtypes=True))
    p.op("gp", lambda e: e.iota(pbuf2[0][:, 0:c.JT], [[0, c.JT]], channel_multiplier=1,
                                allow_small_or_imprecise_dtypes=True))
    p.op("gp", lambda e: e.iota(pbuf2[1][:, 0:c.JT], [[1, c.JT]], channel_multiplier=0,
                                allow_small_or_imprecise_dtypes=True))
    gpc = p.tick("gp", "s_gd")

    p.wait("dve", "s_gd", gpc)
    for t_ in (ones_bf, ones_f16, onec_bf):
        p.fop("dve", lambda e, t=t_: e.memset(t[:, :], 0.0))
        p.fop("dve", lambda e, t=t_: e.tensor_scalar_add(out=t[:, :], in0=t[:, :],
                                                        scalar1=1.0))
    p.fop("dve", lambda e: e.memset(bsrc_a[:, :], 0.0))
    p.fop("dve", lambda e: e.memset(bsrc_b[:, :], 0.0))
    for v in range(c.NV):
        p.fop("dve", lambda e, v=v: e.memset(pbuf[v][:, :], 0.0))
    p.fop("dve", lambda e: e.tensor_tensor(out=tri01[:, :], in0=pbuf2[0][:, 0:c.JT],
                                          in1=pbuf2[1][:, 0:c.JT], op=ALU.is_le))
    p.fop("dve", lambda e: e.memset(rowbuf[:, :], 0.0))
    p.fop("dve", lambda e: e.tensor_single_scalar(out=tsc[:, 0:1], in_=iot[:, 0:1],
                                                  scalar=0.0, op=ALU.is_equal))
    p.fop("dve", lambda e: e.tensor_scalar(out=sel0[:, :], in0=ones_f16[:, :],
                                           scalar1=tsc[:, 0:1], scalar2=None,
                                           op0=ALU.mult))
    p.fop("dve", lambda e: e.tensor_single_scalar(out=tsc[:, 0:1], in_=iot[:, 0:1],
                                                  scalar=32.0, op=ALU.is_equal))
    p.fop("dve", lambda e: e.tensor_scalar(out=sel32[:, :], in0=ones_f16[:, :],
                                           scalar1=tsc[:, 0:1], scalar2=None,
                                           op0=ALU.mult))

    p.fop("dve", lambda e: e.tensor_single_scalar(out=tsc[:, 0:1], in_=iot[:, 0:1],
                                                 scalar=63.5, op=ALU.is_gt))
    p.fop("dve", lambda e: e.scalar_tensor_tensor(out=tsc[:, 1:2], in0=tsc[:, 0:1],
                                                 scalar=-64.0, in1=iot[:, 0:1],
                                                 op0=ALU.mult, op1=ALU.add))
    p.fop("dve", lambda e: e.tensor_single_scalar(out=tsc[:, 2:3], in_=tsc[:, 1:2],
                                                 scalar=-c_hi, op=ALU.mult))
    p.fop("dve", lambda e: e.tensor_single_scalar(out=tsc[:, 3:4], in_=tsc[:, 1:2],
                                                 scalar=-c_lo, op=ALU.mult))
    p.fop("dve", lambda e: e.tensor_add(out=tsc[:, 2:3], in0=tsc[:, 2:3], in1=tsc[:, 3:4]))
    dv = p.tick("dve", "s_dve")
    p.wait("act", "s_dve", dv)
    p.op("act", lambda e: e.activation(out=tsc[:, 4:5], in_=tsc[:, 2:3], func=AF.Exp))
    av = p.tick("act", "s_act")
    p.wait("dve", "s_act", av)
    SC = [1.0, -1.0 / 6, 1.0 / 120, -1.0 / 5040, 1.0 / 362880, -1.0 / 39916800,
          1.0 / 6227020800]
    CC = [1.0, -0.5, 1.0 / 24, -1.0 / 720, 1.0 / 40320, -1.0 / 3628800,
          1.0 / 479001600, -1.0 / 87178291200]

    def _horner_step(col, coeff):
        hi = float(np.float32(coeff))
        lo = coeff - hi
        p.fop("dve", lambda e, col=col, hi=hi: e.tensor_scalar(
            out=tsc[:, col:col + 1], in0=tsc[:, col:col + 1],
            scalar1=tsc[:, 6:7], scalar2=hi, op0=ALU.mult, op1=ALU.add))
        if lo != 0.0:
            p.fop("dve", lambda e, col=col, lo=lo: e.tensor_single_scalar(
                out=tsc[:, col:col + 1], in_=tsc[:, col:col + 1],
                scalar=lo, op=ALU.add))

    p.fop("dve", lambda e: e.tensor_mul(out=tsc[:, 6:7], in0=tsc[:, 4:5], in1=tsc[:, 4:5]))
    p.fop("dve", lambda e: e.tensor_scalar(out=tsc[:, 3:4], in0=tsc[:, 6:7],
                                          scalar1=SC[6], scalar2=SC[5],
                                          op0=ALU.mult, op1=ALU.add))
    for ck_ in (SC[4], SC[3], SC[2], SC[1], SC[0]):
        _horner_step(3, ck_)
    p.fop("dve", lambda e: e.tensor_scalar(out=tsc[:, 5:6], in0=tsc[:, 3:4],
                                          scalar1=tsc[:, 4:5], scalar2=None,
                                          op0=ALU.mult))
    p.fop("dve", lambda e: e.tensor_scalar(out=tsc[:, 3:4], in0=tsc[:, 6:7],
                                          scalar1=CC[7], scalar2=CC[6],
                                          op0=ALU.mult, op1=ALU.add))
    for ck_ in (CC[5], CC[4], CC[3], CC[2], CC[1], CC[0]):
        _horner_step(3, ck_)
    p.fop("dve", lambda e: e.tensor_copy(out=tsc[:, 7:8], in_=tsc[:, 3:4]))
    p.fop("dve", lambda e: e.memset(tabS[:, 0:1], 0.0))
    p.fop("dve", lambda e: e.memset(tabC[:, 0:1], 0.0))
    p.fop("dve", lambda e: e.tensor_scalar_add(out=tabC[:, 0:1], in0=tabC[:, 0:1],
                                              scalar1=1.0))
    for k in range(NLEV):
        w = 1 << k
        p.fop("dve", lambda e, w=w: e.tensor_scalar_mul(out=tabC[:, w:2 * w],
                                                       in0=tabS[:, 0:w],
                                                       scalar1=tsc[:, 5:6]))
        p.fop("dve", lambda e, w=w: e.scalar_tensor_tensor(
            out=tabC[:, w:2 * w], in0=tabC[:, 0:w], scalar=tsc[:, 7:8],
            in1=tabC[:, w:2 * w], op0=ALU.mult, op1=ALU.subtract))
        p.fop("dve", lambda e, w=w: e.tensor_scalar_mul(out=tabS[:, w:2 * w],
                                                       in0=tabC[:, 0:w],
                                                       scalar1=tsc[:, 5:6]))
        p.fop("dve", lambda e, w=w: e.scalar_tensor_tensor(
            out=tabS[:, w:2 * w], in0=tabS[:, 0:w], scalar=tsc[:, 7:8],
            in1=tabS[:, w:2 * w], op0=ALU.mult, op1=ALU.add))
        if k + 1 < NLEV:
            p.fop("dve", lambda e: e.tensor_mul(out=tsc[:, 3:4], in0=tsc[:, 5:6],
                                               in1=tsc[:, 7:8]))
            p.fop("dve", lambda e: e.tensor_single_scalar(out=tsc[:, 5:6], in_=tsc[:, 3:4],
                                                         scalar=2.0, op=ALU.mult))
            p.fop("dve", lambda e: e.tensor_mul(out=tsc[:, 3:4], in0=tsc[:, 7:8],
                                               in1=tsc[:, 7:8]))
            p.fop("dve", lambda e: e.tensor_scalar(out=tsc[:, 7:8], in0=tsc[:, 3:4],
                                                  scalar1=2.0, scalar2=-1.0,
                                                  op0=ALU.mult, op1=ALU.add))
            p.fop("dve", lambda e: e.tensor_mul(out=tsc[:, 3:4], in0=tsc[:, 5:6],
                                               in1=tsc[:, 5:6]))
            p.fop("dve", lambda e: e.tensor_mul(out=tsc[:, 6:7], in0=tsc[:, 7:8],
                                               in1=tsc[:, 7:8]))
            p.fop("dve", lambda e: e.tensor_add(out=tsc[:, 3:4], in0=tsc[:, 3:4],
                                               in1=tsc[:, 6:7]))
            p.fop("dve", lambda e: e.tensor_scalar(out=tsc[:, 3:4], in0=tsc[:, 3:4],
                                                  scalar1=-0.5, scalar2=1.5,
                                                  op0=ALU.mult, op1=ALU.add))
            p.fop("dve", lambda e: e.tensor_scalar(out=tsc[:, 5:6], in0=tsc[:, 5:6],
                                                  scalar1=tsc[:, 3:4], scalar2=None,
                                                  op0=ALU.mult))
            p.fop("dve", lambda e: e.tensor_scalar(out=tsc[:, 7:8], in0=tsc[:, 7:8],
                                                  scalar1=tsc[:, 3:4], scalar2=None,
                                                  op0=ALU.mult))
    p.fop("dve", lambda e: e.tensor_copy(out=cosT[:, :], in_=tabC[:, :]))
    p.fop("dve", lambda e: e.tensor_single_scalar(out=sinT[0:64, :], in_=tabS[0:64, :],
                                                 scalar=-1.0, op=ALU.mult))
    p.fop("dve", lambda e: e.tensor_copy(out=sinT[64:128, :], in_=tabS[64:128, :]))
    consts_done = p.tick("dve", "s_dve")
    if "dbg" in AB:
        p.wait("sp", "s_dve", consts_done)
        dma("sp", dbg_hbm[:, 56:64], tsc[:, :], "s_db")
        p.op("sp", lambda e: e.dma_start(out=dbg_hbm[:, 24:25], in_=iot[:, 0:1],
                                         allow_slow_non_contiguous=True),
             inc="s_db", n=16)
        dma("gp", dbg_hbm[:, 40:48], cosT[:, 0:8], "s_db")
        dma("gp", dbg_hbm[:, 48:56], sinT[:, 0:8], "s_db")
        p.wait("gp", "s_db", p.now("s_db"))

    x_loaded0 = dma("sp", x_sb[:, :, :],
                    x_hbm[0].rearrange("(k p) t -> p k t", p=128), "s_ld")
    p.op("sp", lambda e: e.dma_start(out=mu_sb[:, 0:CH],
                                     in_=mur_h.rearrange("(k p) -> p k", p=128),
                                     allow_slow_non_contiguous=True), inc="s_mu", n=16)
    mu_done = p.op("sp", lambda e: e.dma_start(
        out=mu_sb[:, CH:2 * CH], in_=muk_h.rearrange("(k p) -> p k", p=128),
        allow_slow_non_contiguous=True), inc="s_mu", n=16)
    p.wait("dve", "s_mu", mu_done)


    xbf_guard = [0]
    xsq_guard = [0]
    xn2_guard = [0]

    def settle_dve2():
        for _ in range(3):
            p.fop("dve", lambda e: e.tensor_copy(out=tsc[:, 4:8], in_=tsc[:, 0:4]))

    def settle_dve():
        for _ in range(3):
            p.op("dve", lambda e: e.tensor_copy(out=xsq_t[:, 1, 0:64],
                                                in_=xsq_t[:, 0, 0:64]))

    def ln_stats(t0, xdep, k):
        p.wait("gp", "s_pe", xbf_guard[0])
        if xdep:
            p.wait("gp", xdep[0], xdep[1])
        ca = dma("gp", xbf_t[:, :, :], x_sb[:, :, t0:t0 + c.TT], "s_xc")
        p.wait("dve", "s_xc", ca)
        p.wait("dve", "s_rb", p.now("s_rb"))
        p.wait("dve", "s_pe", xsq_guard[0])
        p.op("dve", lambda e: e.tensor_mul(out=xsq_t[:, :, :], in0=xbf_t[:, :, :],
                                           in1=xbf_t[:, :, :]))
        sqd = p.tick("dve", "s_dve")
        pe_acquire(0)
        pe_acquire(1)
        p.wait("pe", "s_xc", ca)
        p.wait("pe", "s_dve", sqd)
        for ck in range(CH):
            p.op("pe", lambda e, ck=ck: e.matmul(
                out=pb[0][:, 0:c.TT], lhsT=ones_bf[:, :], rhs=xbf_t[:, ck, :],
                start=(ck == 0), stop=(ck == CH - 1)))
        for ck in range(CH):
            p.op("pe", lambda e, ck=ck: e.matmul(
                out=pb[1][:, 0:c.TT], lhsT=ones_bf[:, :], rhs=xsq_t[:, ck, :],
                start=(ck == 0), stop=(ck == CH - 1)))
        std = p.tick("pe", "s_pe")
        xbf_guard[0] = std
        xsq_guard[0] = std
        p.wait("dve", "s_pe", std)
        settle_dve()
        sm = stash[:, k * c.TT:(k + 1) * c.TT]
        sv = stash[:, (2 + k) * c.TT:(3 + k) * c.TT]
        p.op("dve", lambda e, sm=sm: e.tensor_scalar(
            out=sm, in0=pb[0][:, 0:c.TT], scalar1=1.0 / D,
            scalar2=None, op0=ALU.mult))
        p.op("dve", lambda e: e.tensor_scalar(out=rsc[:, 0:c.TT],
                                              in0=pb[1][:, 0:c.TT],
                                              scalar1=1.0 / D, scalar2=float(EPS),
                                              op0=ALU.mult, op1=ALU.add))
        bce = p.tick("dve", "s_dve")
        bank_release(0, "s_dve", bce)
        bank_release(1, "s_dve", bce)
        p.op("dve", lambda e, sm=sm: e.scalar_tensor_tensor(
            out=a128[:, 0:c.TT], in0=sm, scalar=-1.0, in1=sm,
            op0=ALU.mult, op1=ALU.mult))
        p.op("dve", lambda e, sv=sv: e.tensor_add(out=sv, in0=rsc[:, 0:c.TT],
                                                  in1=a128[:, 0:c.TT]))

    def ln_pair(t0, xdep):
        ln_stats(t0, xdep, 0)
        ln_stats(t0 + c.TT, xdep, 1)
        W = 2 * c.TT
        svp = stash[:, 2 * c.TT:4 * c.TT]
        smp = stash[:, 0:2 * c.TT]
        p.fop("dve", lambda e, svp=svp: e.tensor_copy(
            out=mgk.bitcast(F32)[:, 0:W], in_=svp))
        p.fop("dve", lambda e: e.tensor_single_scalar(
            out=mgk[:, 0:W], in_=mgk[:, 0:W], scalar=1,
            op=ALU.arith_shift_right))
        p.fop("dve", lambda e: e.tensor_scalar(
            out=mgk[:, 0:W], in0=mgk[:, 0:W], scalar1=-1, scalar2=0x5f3759df,
            op0=ALU.mult, op1=ALU.add))
        p.fop("dve", lambda e: e.tensor_copy(out=a128[:, 0:W],
                                             in_=mgk.bitcast(F32)[:, 0:W]))
        for _it in range(3):
            p.op("dve", lambda e: e.tensor_mul(out=b128[:, 0:W], in0=a128[:, 0:W],
                                               in1=a128[:, 0:W]))
            p.op("dve", lambda e, svp=svp: e.tensor_mul(out=b128[:, 0:W],
                                                        in0=b128[:, 0:W], in1=svp))
            p.op("dve", lambda e: e.tensor_scalar(out=b128[:, 0:W],
                                                  in0=b128[:, 0:W], scalar1=-0.5,
                                                  scalar2=1.5, op0=ALU.mult,
                                                  op1=ALU.add))
            p.op("dve", lambda e: e.tensor_mul(out=a128[:, 0:W], in0=a128[:, 0:W],
                                               in1=b128[:, 0:W]))
        p.op("dve", lambda e, smp=smp: e.scalar_tensor_tensor(
            out=b128[:, 0:W], in0=smp, scalar=-1.0, in1=a128[:, 0:W],
            op0=ALU.mult, op1=ALU.mult))
        ab_done = p.tick("dve", "s_dve")
        if "norb" not in AB:
            p.wait("sp", "s_dve", ab_done)
            dma("sp", rowbuf[0:1, t0:t0 + W], a128[0:1, 0:W], "s_rb")
            dma("sp", rowbuf[32:33, t0:t0 + W], b128[0:1, 0:W], "s_rb")
        return ab_done

    def bcast_ab(rows_dep, width, from_rowbuf=None):
        t0 = from_rowbuf
        p.wait("pe", "s_dve", rows_dep)
        p.wait("pe", "s_rb", p.now("s_rb"))
        pe_acquire(1)
        p.op("pe", lambda e, t0=t0, w=width: e.matmul(
            out=pb[1][:, 0:w], lhsT=sel0[:, :], rhs=rowbuf[:, t0:t0 + w],
            start=True, stop=True))
        bc1 = p.tick("pe", "s_pe")
        p.wait("dve", "s_pe", bc1)
        p.wait("dve", "s_rb", p.now("s_rb"))
        settle_dve2()
        p.op("dve", lambda e, w=width: e.tensor_copy(out=a128[:, 0:w], in_=pb[1][:, 0:w]))
        ev1 = p.tick("dve", "s_dve")
        bank_release(1, "s_dve", ev1)
        pe_acquire(1)
        p.op("pe", lambda e, t0=t0, w=width: e.matmul(
            out=pb[1][:, 0:w], lhsT=sel32[:, :], rhs=rowbuf[:, t0:t0 + w],
            start=True, stop=True))
        bc2 = p.tick("pe", "s_pe")
        p.wait("dve", "s_pe", bc2)
        settle_dve2()
        p.op("dve", lambda e, w=width: e.tensor_copy(out=b128[:, 0:w], in_=pb[1][:, 0:w]))
        ev2 = p.tick("dve", "s_dve")
        bank_release(1, "s_dve", ev2)
        return ev2

    def apply_ln(dst, t0, width, bc_dep, a_off=0):
        p.wait("dve", "s_dve", bc_dep)
        if "apply" in AB:
            return p.tick("dve", "s_dve")
        for ck in range(CH):
            p.op("dve", lambda e, ck=ck, t0=t0, w=width, o=a_off: e.tensor_mul(
                out=dst[:, ck, 0:w], in0=x_sb[:, ck, t0:t0 + w], in1=a128[:, o:o + w]))
        for ck in range(CH):
            p.op("dve", lambda e, ck=ck, w=width, o=a_off: e.tensor_add(
                out=dst[:, ck, 0:w], in0=dst[:, ck, 0:w], in1=b128[:, o:o + w]))
        return p.tick("dve", "s_dve")

    def ffn_sweep(slot, wdep, src, post):
        p.wait("pe", wdep[0], wdep[1])
        last = 0
        for dk in range(CH):
            for sub in range(c.nST):
                bi = 2 + ((dk * c.nST + sub) % 4)
                pe_acquire(bi)
                for ck in range(CH):
                    p.op("pe", lambda e, dk=dk, ck=ck, sub=sub, bi=bi, slot=slot, src=src:
                         e.matmul(out=pb[bi][:, 0:c.ST],
                                  lhsT=wsl[slot][:, ck, dk, :],
                                  rhs=src[:, ck, sub * c.ST:(sub + 1) * c.ST],
                                  start=(ck == 0), stop=(ck == CH - 1)))
                last = p.tick("pe", "s_pe")
                post(dk, sub, bi, last)
        return last

    ffn_half0_done = [0]

    def ffn_phase(seq, sched, xdep, lim=None, tail_work=None):
        lim = lim or stage
        for half in range(2):
            if half == 1:
                ffn_half0_done[0] = p.now("s_dve")
            h0 = half * c.HB
            if half == 0:
                p.wait("dve", xdep[0], xdep[1])
                p.op("dve", lambda e: e.memset(ncar[:, :, :], 0.0))
                p.tick("dve", "s_dve")
            ntt = c.HB // c.TT
            assert ntt % 2 == 0
            for ti in range(ntt):
                t0 = h0 + ti * c.TT
                if ti % 2 == 0:
                    pair_bc = ln_pair(t0, xdep)
                bc = pair_bc
                if "dbgs" in AB and t0 == 0:
                    p.wait("sp", "s_dve", bc)
                    dma("sp", dbg_hbm[:, 0:16], rsc[:, 64:80], "s_db")
                    dma("sp", dbg_hbm[:, 16:32], rsc[:, c.TT + 64:c.TT + 80], "s_db")
                    dma("gp", dbg_hbm[:, 32:48], a128[:, 64:80], "s_db")
                    dma("gp", dbg_hbm[:, 48:64], b128[:, 64:80], "s_db")
                    p.wait("dve", "s_db", p.now("s_db"))
                apply_ln(xn_t, t0, c.TT, bc, a_off=(ti % 2) * c.TT)
                if "mix" in AB:
                    continue
                p.wait("dve", "s_pe", xsq_guard[0])
                for ck in range(CH):
                    p.op("dve", lambda e, ck=ck: e.tensor_sub(
                        out=xsq_t[:, ck, 1:c.TT], in0=xn_t[:, ck, 1:c.TT],
                        in1=xn_t[:, ck, 0:c.TT - 1]))
                    p.op("dve", lambda e, ck=ck: e.tensor_sub(
                        out=xsq_t[:, ck, 0:1], in0=xn_t[:, ck, 0:1], in1=ncar[:, ck, 0:1]))
                o = ti * c.TT
                for ck in range(CH):
                    p.op("dve", lambda e, ck=ck, o=o: e.scalar_tensor_tensor(
                        out=xr_sb[:, ck, o + 1:o + c.TT], in0=xsq_t[:, ck, 1:c.TT],
                        scalar=mu_sb[:, ck:ck + 1], in1=xn_t[:, ck, 0:c.TT - 1],
                        op0=ALU.mult, op1=ALU.add))
                    p.op("dve", lambda e, ck=ck, o=o: e.scalar_tensor_tensor(
                        out=xr_sb[:, ck, o:o + 1], in0=xsq_t[:, ck, 0:1],
                        scalar=mu_sb[:, ck:ck + 1], in1=ncar[:, ck, 0:1],
                        op0=ALU.mult, op1=ALU.add))
                    p.op("dve", lambda e, ck=ck, o=o: e.scalar_tensor_tensor(
                        out=xk_sb[:, ck, o + 1:o + c.TT], in0=xsq_t[:, ck, 1:c.TT],
                        scalar=mu_sb[:, CH + ck:CH + ck + 1], in1=xn_t[:, ck, 0:c.TT - 1],
                        op0=ALU.mult, op1=ALU.add))
                    p.op("dve", lambda e, ck=ck, o=o: e.scalar_tensor_tensor(
                        out=xk_sb[:, ck, o:o + 1], in0=xsq_t[:, ck, 0:1],
                        scalar=mu_sb[:, CH + ck:CH + ck + 1], in1=ncar[:, ck, 0:1],
                        op0=ALU.mult, op1=ALU.add))
                for ck in range(CH):
                    p.op("dve", lambda e, ck=ck: e.tensor_copy(
                        out=ncar[:, ck, 0:1], in_=xn_t[:, ck, c.TT - 1:c.TT]))
            mix_done = p.tick("dve", "s_dve")
            p.wait("pe", "s_dve", mix_done)
            if lim == "p1":
                continue

            def post_r(dk, sub, bi, mm):
                if "relu" in AB:
                    p.wait("act", "s_pe", mm)
                    p.op("act", lambda e, dk=dk, sub=sub, bi=bi: e.activation(
                        out=r_sb[:, dk, sub * c.ST:(sub + 1) * c.ST],
                        in_=pb[bi][:, 0:c.ST], func=AF.Relu))
                    bank_release(bi, "s_act", p.tick("act", "s_act"))
                    return
                if "sbact" in AB:
                    p.wait("dve", "s_pe", mm)
                    p.op("dve", lambda e, dk=dk, sub=sub, bi=bi: e.tensor_copy(
                        out=r_sb[:, dk, sub * c.ST:(sub + 1) * c.ST],
                        in_=pb[bi][:, 0:c.ST]))
                    dc = p.tick("dve", "s_dve")
                    bank_release(bi, "s_dve", dc)
                    p.wait("act", "s_dve", dc)
                    p.op("act", lambda e, dk=dk, sub=sub: e.activation(
                        out=r_sb[:, dk, sub * c.ST:(sub + 1) * c.ST],
                        in_=r_sb[:, dk, sub * c.ST:(sub + 1) * c.ST],
                        func=AF.Tanh, scale=0.5))
                    p.tick("act", "s_act")
                    return
                if "sig" in AB:
                    p.wait("dve", "s_pe", mm)
                    p.op("dve", lambda e, dk=dk, sub=sub, bi=bi: e.tensor_copy(
                        out=r_sb[:, dk, sub * c.ST:(sub + 1) * c.ST],
                        in_=pb[bi][:, 0:c.ST]))
                    bank_release(bi, "s_dve", p.tick("dve", "s_dve"))
                    return
                p.wait("act", "s_pe", mm)
                p.op("act", lambda e, dk=dk, sub=sub, bi=bi: e.activation(
                    out=r_sb[:, dk, sub * c.ST:(sub + 1) * c.ST], in_=pb[bi][:, 0:c.ST],
                    func=AF.Tanh, scale=0.5))
                ac = p.tick("act", "s_act")
                bank_release(bi, "s_act", ac)
                p.wait("dve", "s_act", ac)
                p.op("dve", lambda e, dk=dk, sub=sub: e.tensor_scalar(
                    out=r_sb[:, dk, sub * c.ST:(sub + 1) * c.ST],
                    in0=r_sb[:, dk, sub * c.ST:(sub + 1) * c.ST],
                    scalar1=0.5, scalar2=0.5, op0=ALU.mult, op1=ALU.add))
                p.tick("dve", "s_dve")

            def post_k(dk, sub, bi, mm):
                p.wait("act", "s_pe", mm)
                p.op("act", lambda e, dk=dk, sub=sub, bi=bi: e.activation(
                    out=ka_sb[:, dk, sub * c.ST:(sub + 1) * c.ST], in_=pb[bi][:, 0:c.ST],
                    func=AF.Relu))
                ac = p.tick("act", "s_act")
                bank_release(bi, "s_act", ac)
                p.wait("dve", "s_act", ac)
                p.op("dve", lambda e, dk=dk, sub=sub: e.tensor_mul(
                    out=ka_sb[:, dk, sub * c.ST:(sub + 1) * c.ST],
                    in0=ka_sb[:, dk, sub * c.ST:(sub + 1) * c.ST],
                    in1=ka_sb[:, dk, sub * c.ST:(sub + 1) * c.ST]))
                p.tick("dve", "s_dve")

            def post_v(dk, sub, bi, mm):
                p.wait("dve", "s_pe", mm)
                t0v = h0 + sub * c.ST
                p.op("dve", lambda e, dk=dk, sub=sub, bi=bi: e.tensor_mul(
                    out=xn_t.rearrange("p a b -> p (a b)")[:, 0:c.ST],
                    in0=r_sb[:, dk, sub * c.ST:(sub + 1) * c.ST], in1=pb[bi][:, 0:c.ST]))
                g = p.tick("dve", "s_dve")
                bank_release(bi, "s_dve", g)
                p.op("dve", lambda e, dk=dk, t0v=t0v: e.tensor_add(
                    out=x_sb[:, dk, t0v:t0v + c.ST], in0=x_sb[:, dk, t0v:t0v + c.ST],
                    in1=xn_t.rearrange("p a b -> p (a b)")[:, 0:c.ST]))
                p.tick("dve", "s_dve")

            slot, wdep = sched(3 * half + 0)
            last = ffn_sweep(slot, wdep, xr_sb, post_r)
            slot_guard[slot] = ("s_pe", last)
            if lim == "swr":
                continue
            slot, wdep = sched(3 * half + 1)
            last = ffn_sweep(slot, wdep, xk_sb, post_k)
            slot_guard[slot] = ("s_pe", last)
            if lim == "swk":
                continue
            if half == 1 and tail_work:
                tail_work()
            p.wait("pe", "s_dve", p.now("s_dve"))
            p.wait("pe", "s_act", p.now("s_act"))
            slot, wdep = sched(3 * half + 2)
            last = ffn_sweep(slot, wdep, ka_sb, post_v)
            slot_guard[slot] = ("s_pe", last)
        return ("s_dve", p.now("s_dve"))

    pre_tiles = [0]

    def attn_phase(seq, wk_slot, wk_dep, wq_slot, wq_loader, wvm_dep, wo_slot,
                   wo_loader, xdep):
        assert c.nTT % 2 == 0
        for ti in range(pre_tiles[0], c.nTT, 2):
            ln_pair(ti * c.TT, xdep)
        pre_tiles[0] = 0
        rows_all = p.now("s_dve")

        p.wait("pe", wk_dep[0], wk_dep[1])
        p.wait("pe", wvm_dep[0], wvm_dep[1])
        nhg = max(1, (H * DH) // 512)
        wvc = (H * DH) // nhg
        kvg = {0: xn2_guard[0], 1: 0}
        bc_next = bcast_ab(rows_all, c.TT, from_rowbuf=0)
        for bt in range(c.nTT):
            t0 = bt * c.TT
            buf = xn2_t if bt % 2 == 0 else xn_t
            gk = bt % 2
            bc = bc_next
            p.wait("dve", "s_pe", kvg[gk])
            xnd = apply_ln(buf, t0, c.TT, bc)
            if bt + 1 < c.nTT:
                bc_next = bcast_ab(p.now("s_dve"), c.TT,
                                   from_rowbuf=(bt + 1) * c.TT)
            p.wait("pe", "s_dve", xnd)
            for h in range(H):
                bi = 2 + (h % 2)
                pe_acquire(bi)
                for ck in range(CH):
                    p.op("pe", lambda e, h=h, ck=ck, bi=bi, buf=buf: e.matmul(
                        out=pb[bi][:, 0:c.TT], lhsT=wsl[wk_slot][:, h, ck, :],
                        rhs=buf[:, ck, 0:c.TT], start=(ck == 0), stop=(ck == CH - 1)))
                mm = p.tick("pe", "s_pe")
                p.wait("act", "s_pe", mm)
                p.op("act", lambda e, h=h, t0=t0, bi=bi: e.activation(
                    out=K_sb[:, h, t0:t0 + c.TT], in_=pb[bi][:, 0:c.TT],
                    func=AF.Copy))
                bank_release(bi, "s_act", p.tick("act", "s_act"))
            for jl in range(c.TT // c.JT):
                jj = t0 // c.JT + jl
                for hg in range(nhg):
                    bi = 4 + ((jl * nhg + hg) % 2)
                    pe_acquire(bi)
                    for ck in range(CH):
                        p.op("pe", lambda e, jl=jl, hg=hg, ck=ck, bi=bi, w=wvc,
                             buf=buf: e.matmul(
                                 out=pb[bi][:, 0:w],
                                 lhsT=buf[:, ck, jl * c.JT:(jl + 1) * c.JT],
                                 rhs=wvm_sb[:, ck, hg * w:(hg + 1) * w],
                                 start=(ck == 0), stop=(ck == CH - 1)))
                    mm = p.tick("pe", "s_pe")
                    p.wait("act", "s_pe", mm)
                    p.op("act", lambda e, jj=jj, hg=hg, bi=bi, w=wvc: e.activation(
                        out=V_sb[:, jj, hg * w:(hg + 1) * w], in_=pb[bi][:, 0:w],
                        func=AF.Copy))
                    bank_release(bi, "s_act", p.tick("act", "s_act"))
            kvg[gk] = p.now("s_pe")
        xn2_guard[0] = p.now("s_pe")
        if "bar1" in AB:
            p.barrier()
        p.wait("dve", "s_act", p.now("s_act"))
        kv_built = p.tick("dve", "s_dve")
        slot_guard[wk_slot] = ("s_pe", p.now("s_pe"))
        slot_guard[wq_slot] = ("s_pe", p.now("s_pe"))
        wq_dep = wq_loader()

        xscr = xn_t.rearrange("p a b -> p (a b)") if CH * c.TT >= T else scr
        p.wait("sp", "s_dve", kv_built)
        kd_prev = {}
        for h in range(H):
            buf = scr if h % 2 == 0 else xscr
            back = 2 if xscr is not scr else 1
            if h >= back:
                p.wait("sp", "s_dve", kd_prev[h - back])
            swa = dma("sp", buf[0:64, 0:T], K_sb[64:128, h, :], "s_swA")
            swb = dma("sp", buf[64:128, 0:T], K_sb[0:64, h, :], "s_swB")
            p.wait("dve", "s_swA", swa)
            p.wait("dve", "s_swB", swb)
            p.op("dve", lambda e, h=h: e.tensor_mul(out=K_sb[:, h, :], in0=K_sb[:, h, :],
                                                    in1=cosT[:, :]))
            p.op("dve", lambda e, buf=buf: e.tensor_mul(out=buf[:, 0:T], in0=buf[:, 0:T],
                                                        in1=sinT[:, :]))
            p.op("dve", lambda e, h=h, buf=buf: e.tensor_add(
                out=K_sb[:, h, :], in0=K_sb[:, h, :], in1=buf[:, 0:T]))
            kd_prev[h] = p.tick("dve", "s_dve")
        if "bar2" in AB:
            p.barrier()
        rope_k = p.now("s_dve")

        nsub = T // 512
        for h in range(H):
            p.wait("act", "s_dve", rope_k)
            p.op("act", lambda e, h=h: e.activation(out=scr[:, 0:T],
                                                    in_=K_sb[:, h, :],
                                                    func=AF.Square))
            ks = p.tick("act", "s_act")
            for sub in range(nsub):
                bnk = 6 + (sub % 2)
                wb = (sub % 2) * 512
                pe_acquire(bnk)
                p.wait("pe", "s_act", ks)
                p.op("pe", lambda e, sub=sub, wb=wb: e.matmul(
                    out=pbW[:, wb:wb + 512], lhsT=ones_bf[:, :],
                    rhs=scr[:, sub * 512:(sub + 1) * 512], start=True, stop=True))
                km = p.tick("pe", "s_pe")
                p.wait("dve", "s_pe", km)
                if sub == 0:
                    settle_dve2()
                p.op("dve", lambda e, sub=sub, wb=wb: e.tensor_reduce(
                    out=rsc[:, sub:sub + 1], in_=pbW[:, wb:wb + 512],
                    axis=AX.X, op=ALU.max))
                bank_release(bnk, "s_dve", p.tick("dve", "s_dve"))
            p.fence("dve", "s_dve")
            p.fop("dve", lambda e, h=h, nsub=nsub: e.tensor_reduce(
                out=bq_r[:, h:h + 1], in_=rsc[:, 0:nsub], axis=AX.X, op=ALU.max))
        kmax_done = p.now("s_dve")

        if "dbg" in AB:
            p.wait("sp", "s_dve", kmax_done)
            p.wait("gp", "s_dve", kmax_done)
            dma("sp", dbg_hbm[:, 0:H + 1], negB[:, :], "s_db")
            dma("sp", dbg_hbm[:, 16:16 + 2 * H + 4], bq_r[:, :], "s_db")
            dma("gp", dbg_hbm[:, 40:48], cosT[:, 0:8], "s_db")
            dma("gp", dbg_hbm[:, 48:56], sinT[:, 0:8], "s_db")
            dma("sp", dbg_hbm[:, 56:64], tsc[:, :], "s_db")
            dma("gp", dbg_hbm[:, 25:33], K_sb[:, 0, 100:108], "s_db")
            dma("gp", dbg_hbm[:, 33:37], V_sb[:, 0, 0:4], "s_db")
            dma("gp", dbg_hbm[:, 37:40], O_sb[:, 0, 0:3], "s_db")
            p.wait("gp", "s_db", p.now("s_db"))
        if "bar3" in AB:
            p.barrier()
        wo_dep = wo_loader()
        for b in range(c.nI):
            i0 = b * c.IT
            bc = bcast_ab(p.now("s_dve"), c.IT, from_rowbuf=i0)
            for sub in range(c.IT // c.TT):
                p.wait("dve", "s_pe", xn2_guard[0])
                xnd = apply_ln(xn2_t, i0 + sub * c.TT, c.TT, bc, a_off=sub * c.TT)
                p.wait("pe", "s_dve", xnd)
                p.wait("pe", wq_dep[0], wq_dep[1])
                for h in range(H):
                    bi = 2 + (h % 2)
                    pe_acquire(bi)
                    for ck in range(CH):
                        p.op("pe", lambda e, h=h, ck=ck, bi=bi: e.matmul(
                            out=pb[bi][:, 0:c.TT], lhsT=wsl[wq_slot][:, h, ck, :],
                            rhs=xn2_t[:, ck, 0:c.TT], start=(ck == 0),
                            stop=(ck == CH - 1)))
                    mm = p.tick("pe", "s_pe")
                    p.wait("dve", "s_pe", mm)
                    p.op("dve", lambda e, h=h, sub=sub, bi=bi: e.tensor_copy(
                        out=q_ap(h)[:, sub * c.TT:(sub + 1) * c.TT],
                        in_=pb[bi][:, 0:c.TT]))
                    bank_release(bi, "s_dve", p.tick("dve", "s_dve"))
                xn2_guard[0] = p.now("s_pe")
            q_built = p.now("s_dve")
            for g0 in range(0, H, c.HG):
                p.wait("sp", "s_dve", q_built if g0 == 0 else p.now("s_dve"))
                lanes = []
                for h in range(g0, g0 + c.HG):
                    o = (h - g0) * c.IT
                    la = f"s_sq{(h - g0) * 2}"
                    lb = f"s_sq{(h - g0) * 2 + 1}"
                    lanes.append((la, dma("sp", scr[0:64, o:o + c.IT],
                                          q_ap(h)[64:128, :], la)))
                    lanes.append((lb, dma("sp", scr[64:128, o:o + c.IT],
                                          q_ap(h)[0:64, :], lb)))
                for ln_, cv in lanes:
                    p.wait("dve", ln_, cv)
                for h in range(g0, g0 + c.HG):
                    o = (h - g0) * c.IT
                    p.op("dve", lambda e, h=h, i0=i0: e.tensor_mul(
                        out=q_ap(h)[:, :], in0=q_ap(h)[:, :], in1=cosT[:, i0:i0 + c.IT]))
                    p.op("dve", lambda e, o=o, i0=i0: e.tensor_mul(
                        out=scr[:, o:o + c.IT], in0=scr[:, o:o + c.IT],
                        in1=sinT[:, i0:i0 + c.IT]))
                    p.op("dve", lambda e, h=h, o=o: e.tensor_add(
                        out=q_ap(h)[:, :], in0=q_ap(h)[:, :], in1=scr[:, o:o + c.IT]))
                p.tick("dve", "s_dve")
            q_roped = p.now("s_dve")
            if "bar4" in AB:
                p.barrier()
            p.wait("dve", "s_act", p.now("s_act"))
            for g0 in range(0, H, c.HG):
                qg = q_lo if g0 == 0 else q_hi
                ng = c.HG if g0 == 0 else H - c.HG
                qf = qg.rearrange("p a b -> p (a b)")
                p.wait("act", "s_dve", p.now("s_dve"))
                p.op("act", lambda e, qf=qf, ng=ng: e.activation(
                    out=scr[:, 0:ng * c.IT], in_=qf[:, 0:ng * c.IT],
                    func=AF.Square))
                qs = p.tick("act", "s_act")
                for h in range(g0, g0 + ng):
                    bi = 2 + (h % 2)
                    pe_acquire(bi)
                    p.wait("pe", "s_act", qs)
                    p.op("pe", lambda e, h=h, g0=g0: e.matmul(
                        out=pb[2 + (h % 2)][:, 0:c.IT], lhsT=ones_bf[:, :],
                        rhs=scr[:, (h - g0) * c.IT:(h - g0 + 1) * c.IT],
                        start=True, stop=True))
                    qm = p.tick("pe", "s_pe")
                    p.wait("dve", "s_pe", qm)
                    if h == 0:
                        settle_dve2()
                    p.fop("dve", lambda e, h=h, bi=bi: e.tensor_reduce(
                        out=bq_r[:, H + h:H + h + 1], in_=pb[bi][:, 0:c.IT],
                        axis=AX.X, op=ALU.max))
                    bank_release(bi, "s_dve", p.now("s_dve"))
            p.fop("dve", lambda e: e.tensor_add(out=negB[:, 0:H], in0=bq_r[:, 0:H],
                                                in1=bq_r[:, H:2 * H]))
            p.fop("dve", lambda e: e.tensor_single_scalar(
                out=negB[:, 0:H], in_=negB[:, 0:H], scalar=-0.5, op=ALU.mult))
            negB_done = p.now("s_dve")
            p.wait("pe", "s_dve", q_roped)

            njj = c.NV * b + c.NV
            for h in range(H):
                if h % 2 == 0:
                    pvb, lb, bid_pv, bid_l = pb[4], pb[5], 4, 5
                else:
                    pvb, lb, bid_pv, bid_l = pbW[:, 0:512], pbW[:, 512:1024], 6, 7
                pe_acquire(bid_pv)
                pe_acquire(bid_l)

                def emit_pv(jj, pbu, dep, h=h, pvb=pvb, lb=lb, njj=njj):
                    p.wait("pe", dep[0], dep[1])
                    p.op("pe", lambda e, h=h, jj=jj, pbu=pbu, pvb=pvb, f=(jj == 0),
                         l=(jj == njj - 1): e.matmul(
                        out=pvb[:, 0:c.IT], lhsT=V_sb[:, jj, h * DH:(h + 1) * DH],
                        rhs=pbu[:, :], start=f, stop=l))
                    p.op("pe", lambda e, jj=jj, pbu=pbu, lb=lb, f=(jj == 0),
                         l=(jj == njj - 1): e.matmul(
                        out=lb[:, 0:c.IT], lhsT=ones_bf[:, :], rhs=pbu[:, :],
                        start=f, stop=l))
                    p.tick("pe", "s_pe")

                pend = []
                pbuf2x = rsc.bitcast(BF16)[:, 0:c.IT]
                for jj in range(njj):
                    v = jj - c.NV * b
                    diag = v >= 0
                    sb = (2, 3, 0)[jj % 3]
                    pe_acquire(sb)
                    n0 = v * c.JT if diag else 0
                    w = c.IT - n0
                    p.op("pe", lambda e, h=h, jj=jj, sb=sb, n0=n0, w=w: e.matmul(
                        out=pb[sb][:, 0:w], lhsT=K_sb[:, h, jj * c.JT:(jj + 1) * c.JT],
                        rhs=q_ap(h)[:, n0:c.IT], start=True, stop=True))
                    mm = p.tick("pe", "s_pe")
                    pbu = pbuf[v] if diag else (pbuf2[0], pbuf2[1], pbuf2x)[jj % 3]
                    p.wait("act", "s_pe", mm)
                    p.wait("act", "s_dve", negB_done)
                    p.op("act", lambda e, h=h, sb=sb, n0=n0, w=w, pbu=pbu: e.activation(
                        out=pbu[:, n0:c.IT], in_=pb[sb][:, 0:w], func=AF.Exp,
                        bias=negB[:, h:h + 1]))
                    ex = p.tick("act", "s_act")
                    bank_release(sb, "s_act", ex)
                    if diag:
                        p.wait("dve", "s_act", ex)
                        p.op("dve", lambda e, pbu=pbu, n0=n0: e.tensor_mul(
                            out=pbu[:, n0:n0 + c.JT], in0=pbu[:, n0:n0 + c.JT],
                            in1=tri01[:, :]))
                        exd = p.tick("dve", "s_dve")
                        dep = ("s_dve", exd)
                    else:
                        dep = ("s_act", ex)
                    pend.append((jj, pbu, dep))
                    if len(pend) >= 3:
                        emit_pv(*pend.pop(0))
                for it in pend:
                    emit_pv(*it)
                pvl = p.now("s_pe")
                p.wait("dve", "s_pe", pvl)
                p.op("dve", lambda e, lb=lb: e.reciprocal(out=invl[:, :],
                                                          in_=lb[:, 0:c.IT]))
                p.op("dve", lambda e, h=h, pvb=pvb: e.tensor_mul(out=O_sb[:, h, :],
                                                                 in0=pvb[:, 0:c.IT],
                                                                 in1=invl[:, :]))
                od = p.tick("dve", "s_dve")
                bank_release(bid_pv, "s_dve", od)
                bank_release(bid_l, "s_dve", od)
            o_all = p.now("s_dve")
            if "bar5" in AB:
                p.barrier()
            p.wait("pe", wo_dep[0], wo_dep[1])
            p.wait("pe", "s_dve", o_all)
            for mk in range(CH):
                bnk = 6 + (mk % 2)
                wb = (mk % 2) * 512
                pe_acquire(bnk)
                for h in range(H):
                    p.op("pe", lambda e, mk=mk, h=h, wb=wb: e.matmul(
                        out=pbW[:, wb:wb + c.IT], lhsT=wsl[wo_slot][:, h, mk, :],
                        rhs=O_sb[:, h, :], start=(h == 0), stop=(h == H - 1)))
                mm = p.tick("pe", "s_pe")
                p.wait("dve", "s_pe", mm)
                p.op("dve", lambda e, mk=mk, i0=i0, wb=wb: e.tensor_add(
                    out=x_sb[:, mk, i0:i0 + c.IT], in0=x_sb[:, mk, i0:i0 + c.IT],
                    in1=pbW[:, wb:wb + c.IT]))
                bank_release(bnk, "s_dve", p.tick("dve", "s_dve"))
        if "bar6" in AB:
            p.barrier()
        slot_guard[wq_slot] = ("s_pe", p.now("s_pe"))
        slot_guard[wo_slot] = ("s_pe", p.now("s_pe"))
        return ("s_dve", p.now("s_dve"))

    p.wait("dve", "s_dve", consts_done)
    p.wait("pe", "s_dve", consts_done)
    store_done = 0
    store_h0 = [0]
    for s in range(c.NSEQ):
        if s > 0:
            if store_h0[0]:
                p.wait("sp", "s_st", store_h0[0])
                dma("sp", x_sb[:, :, 0:c.HB],
                    x_hbm[s].rearrange("(k p) t -> p k t", p=128)[:, :, 0:c.HB],
                    "s_ld")
                p.wait("sp", "s_st", store_done)
                xl = dma("sp", x_sb[:, :, c.HB:T],
                         x_hbm[s].rearrange("(k p) t -> p k t", p=128)[:, :, c.HB:T],
                         "s_ld")
            else:
                p.wait("sp", "s_st", store_done)
                xl = dma("sp", x_sb[:, :, :],
                         x_hbm[s].rearrange("(k p) t -> p k t", p=128), "s_ld")
        else:
            xl = x_loaded0
        xdep = ("s_ld", xl)
        p.wait("dve", "s_ld", xl)
        p.wait("pe", "s_ld", xl)

        ffn_slots = [0, 1, 0, 1, 0, 1]
        loaded = {}
        if s == 0:
            loaded[0] = load_ffn("r", 0)
            loaded[1] = load_ffn("k", 1)

        def mk_sched(loaded):
            def sched(idx):
                slot = ffn_slots[idx]
                if idx not in loaded:
                    loaded[idx] = load_ffn("rkv"[idx % 3], slot)
                if idx + 1 < 6 and (idx + 1) not in loaded:
                    loaded[idx + 1] = load_ffn("rkv"[(idx + 1) % 3], ffn_slots[idx + 1])
                return slot, loaded[idx]
            return sched

        if stage != "s0":
            if stage in ("s2", "full") or stage.startswith("f_"):
                def ln2_head():
                    for ti in range(0, c.nTT // 2, 2):
                        ln_pair(ti * c.TT, ("s_dve", ffn_half0_done[0]))
                    pre_tiles[0] = c.nTT // 2
                xdep = ffn_phase(s, mk_sched(loaded), xdep, tail_work=ln2_head)
            else:
                xdep = ffn_phase(s, mk_sched(loaded), xdep)
        if stage in ("p1", "swr", "swk"):
            pass

        if stage not in ("s0", "s1", "p1", "swr", "swk"):
            pass
        if stage in ("s2", "full") or stage.startswith("f_"):
            wk_dep = load_attn("wk", 0)
            wvm_dep = load_attn("wvm", 1)
            xdep = attn_phase(s, 0, wk_dep, 1, lambda: load_attn("wq", 1), wvm_dep, 0,
                              lambda: load_attn("wo", 0), xdep)

        if "dbgx" in AB:
            p.wait("sp", xdep[0], xdep[1])
            dma("sp", dbg_hbm[:, 0:32], x_sb[:, 0, 0:32], "s_db")
            p.wait("gp", "s_db", p.now("s_db"))
        if "dbgy" in AB:
            p.wait("gp", xdep[0], xdep[1] if True else 0)
        if stage.startswith("f_") or stage == "full":
            lim2 = stage[2:] if stage.startswith("f_") else "full"
            xdep = ffn_phase(s, mk_sched({}), xdep, lim=lim2)

        if stage.startswith("f_") or stage == "full":
            p.wait("sp", "s_dve", ffn_half0_done[0])
            store_h0[0] = dma(
                "sp", out_hbm[s].rearrange("(k p) t -> p k t", p=128)[:, :, 0:c.HB],
                x_sb[:, :, 0:c.HB], "s_st")
            p.wait("sp", xdep[0], xdep[1])
            store_done = dma(
                "sp", out_hbm[s].rearrange("(k p) t -> p k t", p=128)[:, :, c.HB:T],
                x_sb[:, :, c.HB:T], "s_st")
        else:
            p.wait("sp", xdep[0], xdep[1])
            store_done = dma("sp", out_hbm[s].rearrange("(k p) t -> p k t", p=128),
                             x_sb[:, :, :], "s_st")

    if "dbge" in AB:
        p.wait("gp", "s_st", store_done)
        dma("gp", dbg_hbm[:, 0:H + 1], negB[:, :], "s_db")
        dma("gp", dbg_hbm[:, 16:16 + c.IT // 32], invl[:, 0:c.IT:32], "s_db") if False \
            else dma("gp", dbg_hbm[:, 16:32], invl[:, 0:16], "s_db")
        dma("gp", dbg_hbm[:, 32:48], O_sb[:, 0, 0:16], "s_db")
        dma("gp", dbg_hbm[:, 48:64], O_sb[:, 1, 0:16], "s_db")
        p.wait("gp", "s_db", p.now("s_db"))
    p.wait("gp", "s_st", store_done)

    with nc.allow_low_precision(reason="f16 row temps are Newton-refined; operand precision is budgeted"), nc.Block() as block:
        @block.tensor
        def _(e):
            for fn in p.q["pe"]:
                fn(e)

        @block.vector
        def _(e):
            for fn in p.q["dve"]:
                fn(e)

        @block.scalar
        def _(e):
            for fn in p.q["act"]:
                fn(e)

        @block.gpsimd
        def _(e):
            for fn in p.q["gp"]:
                fn(e)

        @block.sync
        def _(e):
            for fn in p.q["sp"]:
                fn(e)

    return nc


def kernel(L0_attn_x, L0_attn_wq, L0_attn_wk, L0_attn_wv, L0_attn_wo,
           L0_ffn_W_r, L0_ffn_W_k, L0_ffn_W_v, L0_ffn_mu_r, L0_ffn_mu_k):
    c = Cfg()
    nc = build_nc(c)
    x = np.ascontiguousarray(np.asarray(L0_attn_x, dtype=np.float32))
    B = x.shape[0]
    ncore = 8
    spc = B // ncore
    f32 = lambda a: np.ascontiguousarray(np.asarray(a, dtype=np.float32))
    common = {
        "wq": f32(L0_attn_wq), "wk": f32(L0_attn_wk), "wv": f32(L0_attn_wv),
        "wo": f32(L0_attn_wo),
        "wrT": np.ascontiguousarray(np.asarray(L0_ffn_W_r, dtype=np.float32).T),
        "wkT": np.ascontiguousarray(np.asarray(L0_ffn_W_k, dtype=np.float32).T),
        "wvT": np.ascontiguousarray(np.asarray(L0_ffn_W_v, dtype=np.float32).T),
        "mur": f32(L0_ffn_mu_r), "muk": f32(L0_ffn_mu_k),
    }
    in_maps = []
    for i in range(ncore):
        xs = np.ascontiguousarray(np.transpose(x[i * spc:(i + 1) * spc], (0, 2, 1)))
        in_maps.append(dict(common, x=xs))
    res = run_bass_kernel_spmd(nc, in_maps, core_ids=list(range(ncore)))
    outs = [np.transpose(r["out"], (0, 2, 1)) for r in res.results]
    return np.ascontiguousarray(np.concatenate(outs, axis=0)).astype(np.float32)


# version 17
# speedup vs baseline: 1.0085x; 1.0085x over previous
import math
import numpy as np
import concourse.bass as bass
import concourse.mybir as mybir
from concourse.bass_utils import run_bass_kernel_spmd

F32 = mybir.dt.float32
BF16 = mybir.dt.bfloat16
F16 = mybir.dt.float16
AF = mybir.ActivationFunctionType
ALU = mybir.AluOpType
AX = mybir.AxisListType


class Cfg:
    def __init__(self, T=2048, D=1024, H=8, NSEQ=2, TT=256, IT=512):
        self.T, self.D, self.H, self.NSEQ = T, D, H, NSEQ
        self.DH = 128
        self.CH = D // 128
        self.TT = min(TT, T)
        self.IT = min(IT, T)
        self.JT = 128
        self.NV = self.IT // self.JT
        self.nTT = T // self.TT
        self.nI = T // self.IT
        self.nJ = T // self.JT
        self.HB = T // 2
        self.ST = min(512, self.HB)
        self.nST = self.HB // self.ST
        self.HG = max(1, H // 2)
        assert D == H * self.DH and self.HB % self.ST == 0
        assert self.IT % self.TT == 0


class Prog:
    ENGINES = ("pe", "dve", "act", "gp", "sp")

    def __init__(self, nc):
        self.nc = nc
        self.q = {e: [] for e in self.ENGINES}
        self.sems = {}
        self.cnt = {}
        self.lastw = {}
        self.last_op = {}

    def sem(self, name):
        if name not in self.sems:
            self.sems[name] = self.nc.semaphore(name).__enter__()
            self.cnt[name] = 0
        return self.sems[name]

    def op(self, eng, fn, inc=None, n=1):
        if inc is None:
            self.q[eng].append(fn)
            self.last_op[eng] = len(self.q[eng]) - 1
            return None
        h = self.sem(inc)
        self.cnt[inc] += n
        self.q[eng].append(lambda e, fn=fn, h=h, n=n: fn(e).then_inc(h, n))
        self.last_op[eng] = None
        return self.cnt[inc]

    def tick(self, eng, sem, n=1):
        h = self.sem(sem)
        self.cnt[sem] += n
        li = self.last_op.get(eng)
        if li is not None and li == len(self.q[eng]) - 1:
            fn = self.q[eng][li]
            self.q[eng][li] = lambda e, fn=fn, h=h, n=n: fn(e).then_inc(h, n)
            self.last_op[eng] = None
        else:
            self.q[eng].append(lambda e, h=h, n=n: e.sem_inc(h, n))
        return self.cnt[sem]

    def wait(self, eng, sem, val):
        if not val:
            return
        assert val <= self.cnt.get(sem, 0), \
            f"wait on future {sem}={val} (have {self.cnt.get(sem)})"
        key = (eng, sem)
        if self.lastw.get(key, 0) >= val:
            return
        self.lastw[key] = val
        h = self.sem(sem)
        self.q[eng].append(lambda e, h=h, v=val: e.wait_ge(h, v))

    def now(self, sem):
        self.sem(sem)
        return self.cnt[sem]

    def fop(self, eng, fn):
        self.op(eng, fn)
        self.fence(eng, "s_dve" if eng == "dve" else "s_act")

    def barrier(self):
        for e in ("pe", "dve", "act"):
            self.tick(e, "s_bar")
        v = self.now("s_bar")
        for e in ("pe", "dve", "act", "gp", "sp"):
            self.wait(e, "s_bar", v)

    def fence(self, eng, sem):
        self.wait(eng, sem, self.tick(eng, sem))


def build_nc(c: Cfg, stage="full"):
    import os
    AB = set(os.environ.get("KAB", "").split(","))
    nc = bass.Bass("TRN2", target_bir_lowering=False, detect_race_conditions=False)
    CH, T, H, DH, D = c.CH, c.T, c.H, c.DH, c.D
    A = nc.alloc_sbuf_tensor
    AT = nc.alloc_sbuf_tensor_at

    x_hbm = nc.dram_tensor("x", [c.NSEQ, D, T], F32, kind="ExternalInput").ap()
    wq_h = nc.dram_tensor("wq", [H, D, DH], F32, kind="ExternalInput").ap()
    wk_h = nc.dram_tensor("wk", [H, D, DH], F32, kind="ExternalInput").ap()
    wv_h = nc.dram_tensor("wv", [H, D, DH], F32, kind="ExternalInput").ap()
    wo_h = nc.dram_tensor("wo", [H, DH, D], F32, kind="ExternalInput").ap()
    wrT_h = nc.dram_tensor("wrT", [D, D], F32, kind="ExternalInput").ap()
    wkT_h = nc.dram_tensor("wkT", [D, D], F32, kind="ExternalInput").ap()
    wvT_h = nc.dram_tensor("wvT", [D, D], F32, kind="ExternalInput").ap()
    mur_h = nc.dram_tensor("mur", [D], F32, kind="ExternalInput").ap()
    muk_h = nc.dram_tensor("muk", [D], F32, kind="ExternalInput").ap()
    out_hbm = nc.dram_tensor("out", [c.NSEQ, D, T], F32, kind="ExternalOutput").ap()
    dbg_hbm = nc.dram_tensor("dbg", [128, 64], F32, kind="ExternalOutput").ap() \
        if any(k in AB for k in ("dbg", "dbgx", "dbge", "dbgs")) else None

    K_sb = A("K_sb", [128, H, T], BF16)
    V_sb = A("V_sb", [128, c.nJ, H * DH], BF16)
    big_bytes = 2 * H * T * 2
    big = nc.sbuf_base - big_bytes
    assert 3 * CH * c.HB * 2 <= big_bytes and 2 * T * 4 <= big_bytes
    xr_sb = AT("xr_sb", [128, CH, c.HB], BF16, offset=big)
    xk_sb = AT("xk_sb", [128, CH, c.HB], BF16, offset=big + CH * c.HB * 2)
    r_sb = AT("r_sb", [128, CH, c.HB], BF16, offset=big + 2 * CH * c.HB * 2)
    ka_sb = AT("ka_sb", [128, CH, c.HB], BF16, offset=big)
    tabC = AT("tabC", [128, T], F32, offset=big)
    tabS = AT("tabS", [128, T], F32, offset=big + T * 4)
    assert 3 * CH * c.HB * 2 + 4 * c.TT * 2 <= big_bytes
    stash = AT("stash", [128, 4 * c.TT], F16, offset=big + 3 * CH * c.HB * 2)

    x_sb = A("x_sb", [128, CH, T], F32)
    wsl = [A(f"wsl{i}", [128, CH, CH, 128], BF16) for i in range(2)]
    wsl1_base = nc.sbuf_base - CH * CH * 128 * 2
    wvm_sb = AT("wvm", [128, CH, H * DH], BF16, offset=wsl1_base)

    SB = c.TT * CH * 2
    regS = A("regS", [128, CH * c.TT * 2], BF16)
    s_base = nc.sbuf_base - 2 * SB
    xbf_t = AT("xbf", [128, CH, c.TT], BF16, offset=s_base)
    xsq_t = AT("xsq", [128, CH, c.TT], BF16, offset=s_base + SB)
    scr_cols = max(T, c.HG * c.IT, c.TT * CH)
    if SB // 2 >= scr_cols:
        scr = AT("scr", [128, SB // 2], BF16, offset=s_base + SB)
    else:
        scr = A("scr", [128, scr_cols], BF16)
    O_sb = AT("O_sb", [128, H, c.IT], BF16, offset=s_base) \
        if H * c.IT * 2 <= 2 * SB else A("O_sb", [128, H, c.IT], BF16)

    xn_t = A("xn", [128, CH, c.TT], BF16)
    xn_base = nc.sbuf_base - CH * c.TT * 2
    xn2_t = A("xn2", [128, CH, c.TT], BF16)
    q_lo = A("q_lo", [128, c.HG, c.IT], BF16)
    assert (H - c.HG) * c.IT * 2 <= CH * c.TT * 2
    q_hi = AT("q_hi", [128, H - c.HG, c.IT], BF16, offset=xn_base)
    ncar = A("ncar", [128, CH, 1], BF16)
    pbuf = [A(f"pbuf{v}", [128, c.IT], BF16) for v in range(c.NV)]
    pbuf2 = [A(f"pbuf2_{i}", [128, c.IT], BF16) for i in range(2)]
    invl = A("invl", [128, c.IT], F32)
    mgk = AT("mgk", [128, c.IT], mybir.dt.int32, offset=nc.sbuf_base - c.IT * 4)

    bsrc_a = A("bsrc_a", [128, c.IT], F16)
    bsrc_b = A("bsrc_b", [128, c.IT], F16)
    a128 = A("a128", [128, c.IT], F16)
    b128 = A("b128", [128, c.IT], F16)
    rowbuf = A("rowbuf", [128, T], F16)
    rsc = A("rsc", [128, 2 * c.TT], F32)
    bq_r = A("bq_r", [128, 2 * H + 4], F32)
    negB = A("negB", [128, H + 1], F32)

    cosT = A("cosT", [128, T], F16)
    sinT = A("sinT", [128, T], F16)
    ones_bf = A("ones_bf", [128, 128], BF16)
    ones_f16 = A("ones_f16", [128, 128], F16)
    sel0 = A("sel0", [128, 128], F16)
    sel32 = A("sel32", [128, 128], F16)
    onec_bf = A("onec_bf", [128, 1], BF16)
    tri01 = A("tri01", [128, c.JT], BF16)
    mu_sb = A("mu_sb", [128, 4 * CH], F32)
    iot = A("iot", [128, 1], F32)
    tsc = A("tsc", [128, 8], F32)

    pb = [nc.alloc_psum_tensor(f"pb{i}", [128, 512], F32) for i in range(6)]
    pbW = nc.alloc_psum_tensor("pbW", [128, 1024], F32)

    def q_ap(h):
        return q_lo[:, h, :] if h < c.HG else q_hi[:, h - c.HG, :]

    p = Prog(nc)
    LOG1E4_64 = math.log(10000.0) / 64.0
    c_hi = float(np.float32(LOG1E4_64))
    c_lo = float(np.float32(np.float64(LOG1E4_64) - np.float64(np.float32(LOG1E4_64))))
    EPS = 1e-5
    NLEV = int(round(math.log2(T)))
    assert (1 << NLEV) == T

    bank_guard = {}

    def pe_acquire(bi):
        g = bank_guard.get(bi)
        if g:
            p.wait("pe", g[0], g[1])

    def bank_release(bi, sem, cnt):
        bank_guard[bi] = (sem, cnt)

    def dma(q, out_ap, in_ap, sem):
        return p.op(q, lambda e, o=out_ap, i=in_ap: e.dma_start(out=o, in_=i),
                    inc=sem, n=16)

    slot_guard = {0: None, 1: None}

    def load_slot(slot, out_ap, in_ap):
        g = slot_guard[slot]
        if g:
            p.wait("gp", g[0], g[1])
        return dma("gp", out_ap, in_ap, f"s_w{slot}")

    W_FFN = {"r": wrT_h, "k": wkT_h, "v": wvT_h}

    def load_ffn(kind, slot):
        cnt = load_slot(slot, wsl[slot][:, :, :, :],
                        W_FFN[kind].rearrange("(ck p) (dk d) -> p ck dk d", p=128, d=128))
        if kind == "v":
            p.wait("dve", f"s_w{slot}", cnt)
            p.op("dve", lambda e, slot=slot: e.tensor_scalar_mul(
                out=wsl[slot].rearrange("p a b d -> p (a b d)"),
                in0=wsl[slot].rearrange("p a b d -> p (a b d)"), scalar1=0.5))
            return ("s_dve", p.tick("dve", "s_dve"))
        return (f"s_w{slot}", cnt)

    def load_attn(kind, slot):
        if kind == "wo":
            cnt = load_slot(slot, wsl[slot][:, :, :, :],
                            wo_h.rearrange("h p (mk m) -> p h mk m", m=128))
            return (f"s_w{slot}", cnt)
        if kind == "wvm":
            g = slot_guard[slot]
            if g:
                p.wait("gp", g[0], g[1])
            for h in range(H):
                cnt = dma("gp", wvm_sb[:, :, h * DH:(h + 1) * DH],
                          wv_h[h].rearrange("(ck p) d -> p ck d", p=128), f"s_w{slot}")
            return (f"s_w{slot}", cnt)
        src = {"wq": wq_h, "wk": wk_h}[kind]
        cnt = load_slot(slot, wsl[slot][:, :, :, :],
                        src.rearrange("h (ck p) d -> p h ck d", p=128))
        sc_ = float(128.0 ** -0.5)
        p.wait("dve", f"s_w{slot}", cnt)
        p.op("dve", lambda e, slot=slot, sc_=sc_: e.tensor_scalar_mul(
            out=wsl[slot].rearrange("p a b d -> p (a b d)"),
            in0=wsl[slot].rearrange("p a b d -> p (a b d)"),
            scalar1=sc_))
        return ("s_dve", p.tick("dve", "s_dve"))

    p.op("gp", lambda e: e.iota(iot[:, 0:1], [[1, 1]], channel_multiplier=1,
                                allow_small_or_imprecise_dtypes=True))
    p.op("gp", lambda e: e.iota(pbuf2[0][:, 0:c.JT], [[0, c.JT]], channel_multiplier=1,
                                allow_small_or_imprecise_dtypes=True))
    p.op("gp", lambda e: e.iota(pbuf2[1][:, 0:c.JT], [[1, c.JT]], channel_multiplier=0,
                                allow_small_or_imprecise_dtypes=True))
    gpc = p.tick("gp", "s_gd")

    p.wait("dve", "s_gd", gpc)
    for t_ in (ones_bf, ones_f16, onec_bf):
        p.fop("dve", lambda e, t=t_: e.memset(t[:, :], 0.0))
        p.fop("dve", lambda e, t=t_: e.tensor_scalar_add(out=t[:, :], in0=t[:, :],
                                                        scalar1=1.0))
    p.fop("dve", lambda e: e.memset(bsrc_a[:, :], 0.0))
    p.fop("dve", lambda e: e.memset(bsrc_b[:, :], 0.0))
    for v in range(c.NV):
        p.fop("dve", lambda e, v=v: e.memset(pbuf[v][:, :], 0.0))
    p.fop("dve", lambda e: e.tensor_tensor(out=tri01[:, :], in0=pbuf2[0][:, 0:c.JT],
                                          in1=pbuf2[1][:, 0:c.JT], op=ALU.is_le))
    p.fop("dve", lambda e: e.memset(rowbuf[:, :], 0.0))
    p.fop("dve", lambda e: e.tensor_single_scalar(out=tsc[:, 0:1], in_=iot[:, 0:1],
                                                  scalar=0.0, op=ALU.is_equal))
    p.fop("dve", lambda e: e.tensor_scalar(out=sel0[:, :], in0=ones_f16[:, :],
                                           scalar1=tsc[:, 0:1], scalar2=None,
                                           op0=ALU.mult))
    p.fop("dve", lambda e: e.tensor_single_scalar(out=tsc[:, 0:1], in_=iot[:, 0:1],
                                                  scalar=32.0, op=ALU.is_equal))
    p.fop("dve", lambda e: e.tensor_scalar(out=sel32[:, :], in0=ones_f16[:, :],
                                           scalar1=tsc[:, 0:1], scalar2=None,
                                           op0=ALU.mult))

    p.fop("dve", lambda e: e.tensor_single_scalar(out=tsc[:, 0:1], in_=iot[:, 0:1],
                                                 scalar=63.5, op=ALU.is_gt))
    p.fop("dve", lambda e: e.scalar_tensor_tensor(out=tsc[:, 1:2], in0=tsc[:, 0:1],
                                                 scalar=-64.0, in1=iot[:, 0:1],
                                                 op0=ALU.mult, op1=ALU.add))
    p.fop("dve", lambda e: e.tensor_single_scalar(out=tsc[:, 2:3], in_=tsc[:, 1:2],
                                                 scalar=-c_hi, op=ALU.mult))
    p.fop("dve", lambda e: e.tensor_single_scalar(out=tsc[:, 3:4], in_=tsc[:, 1:2],
                                                 scalar=-c_lo, op=ALU.mult))
    p.fop("dve", lambda e: e.tensor_add(out=tsc[:, 2:3], in0=tsc[:, 2:3], in1=tsc[:, 3:4]))
    dv = p.tick("dve", "s_dve")
    p.wait("act", "s_dve", dv)
    p.op("act", lambda e: e.activation(out=tsc[:, 4:5], in_=tsc[:, 2:3], func=AF.Exp))
    av = p.tick("act", "s_act")
    p.wait("dve", "s_act", av)
    SC = [1.0, -1.0 / 6, 1.0 / 120, -1.0 / 5040, 1.0 / 362880, -1.0 / 39916800,
          1.0 / 6227020800]
    CC = [1.0, -0.5, 1.0 / 24, -1.0 / 720, 1.0 / 40320, -1.0 / 3628800,
          1.0 / 479001600, -1.0 / 87178291200]

    def _horner_step(col, coeff):
        hi = float(np.float32(coeff))
        lo = coeff - hi
        p.fop("dve", lambda e, col=col, hi=hi: e.tensor_scalar(
            out=tsc[:, col:col + 1], in0=tsc[:, col:col + 1],
            scalar1=tsc[:, 6:7], scalar2=hi, op0=ALU.mult, op1=ALU.add))
        if lo != 0.0:
            p.fop("dve", lambda e, col=col, lo=lo: e.tensor_single_scalar(
                out=tsc[:, col:col + 1], in_=tsc[:, col:col + 1],
                scalar=lo, op=ALU.add))

    p.fop("dve", lambda e: e.tensor_mul(out=tsc[:, 6:7], in0=tsc[:, 4:5], in1=tsc[:, 4:5]))
    p.fop("dve", lambda e: e.tensor_scalar(out=tsc[:, 3:4], in0=tsc[:, 6:7],
                                          scalar1=SC[6], scalar2=SC[5],
                                          op0=ALU.mult, op1=ALU.add))
    for ck_ in (SC[4], SC[3], SC[2], SC[1], SC[0]):
        _horner_step(3, ck_)
    p.fop("dve", lambda e: e.tensor_scalar(out=tsc[:, 5:6], in0=tsc[:, 3:4],
                                          scalar1=tsc[:, 4:5], scalar2=None,
                                          op0=ALU.mult))
    p.fop("dve", lambda e: e.tensor_scalar(out=tsc[:, 3:4], in0=tsc[:, 6:7],
                                          scalar1=CC[7], scalar2=CC[6],
                                          op0=ALU.mult, op1=ALU.add))
    for ck_ in (CC[5], CC[4], CC[3], CC[2], CC[1], CC[0]):
        _horner_step(3, ck_)
    p.fop("dve", lambda e: e.tensor_copy(out=tsc[:, 7:8], in_=tsc[:, 3:4]))
    p.fop("dve", lambda e: e.memset(tabS[:, 0:1], 0.0))
    p.fop("dve", lambda e: e.memset(tabC[:, 0:1], 0.0))
    p.fop("dve", lambda e: e.tensor_scalar_add(out=tabC[:, 0:1], in0=tabC[:, 0:1],
                                              scalar1=1.0))
    for k in range(NLEV):
        w = 1 << k
        p.fop("dve", lambda e, w=w: e.tensor_scalar_mul(out=tabC[:, w:2 * w],
                                                       in0=tabS[:, 0:w],
                                                       scalar1=tsc[:, 5:6]))
        p.fop("dve", lambda e, w=w: e.scalar_tensor_tensor(
            out=tabC[:, w:2 * w], in0=tabC[:, 0:w], scalar=tsc[:, 7:8],
            in1=tabC[:, w:2 * w], op0=ALU.mult, op1=ALU.subtract))
        p.fop("dve", lambda e, w=w: e.tensor_scalar_mul(out=tabS[:, w:2 * w],
                                                       in0=tabC[:, 0:w],
                                                       scalar1=tsc[:, 5:6]))
        p.fop("dve", lambda e, w=w: e.scalar_tensor_tensor(
            out=tabS[:, w:2 * w], in0=tabS[:, 0:w], scalar=tsc[:, 7:8],
            in1=tabS[:, w:2 * w], op0=ALU.mult, op1=ALU.add))
        if k + 1 < NLEV:
            p.fop("dve", lambda e: e.tensor_mul(out=tsc[:, 3:4], in0=tsc[:, 5:6],
                                               in1=tsc[:, 7:8]))
            p.fop("dve", lambda e: e.tensor_single_scalar(out=tsc[:, 5:6], in_=tsc[:, 3:4],
                                                         scalar=2.0, op=ALU.mult))
            p.fop("dve", lambda e: e.tensor_mul(out=tsc[:, 3:4], in0=tsc[:, 7:8],
                                               in1=tsc[:, 7:8]))
            p.fop("dve", lambda e: e.tensor_scalar(out=tsc[:, 7:8], in0=tsc[:, 3:4],
                                                  scalar1=2.0, scalar2=-1.0,
                                                  op0=ALU.mult, op1=ALU.add))
            p.fop("dve", lambda e: e.tensor_mul(out=tsc[:, 3:4], in0=tsc[:, 5:6],
                                               in1=tsc[:, 5:6]))
            p.fop("dve", lambda e: e.tensor_mul(out=tsc[:, 6:7], in0=tsc[:, 7:8],
                                               in1=tsc[:, 7:8]))
            p.fop("dve", lambda e: e.tensor_add(out=tsc[:, 3:4], in0=tsc[:, 3:4],
                                               in1=tsc[:, 6:7]))
            p.fop("dve", lambda e: e.tensor_scalar(out=tsc[:, 3:4], in0=tsc[:, 3:4],
                                                  scalar1=-0.5, scalar2=1.5,
                                                  op0=ALU.mult, op1=ALU.add))
            p.fop("dve", lambda e: e.tensor_scalar(out=tsc[:, 5:6], in0=tsc[:, 5:6],
                                                  scalar1=tsc[:, 3:4], scalar2=None,
                                                  op0=ALU.mult))
            p.fop("dve", lambda e: e.tensor_scalar(out=tsc[:, 7:8], in0=tsc[:, 7:8],
                                                  scalar1=tsc[:, 3:4], scalar2=None,
                                                  op0=ALU.mult))
    p.fop("dve", lambda e: e.tensor_copy(out=cosT[:, :], in_=tabC[:, :]))
    p.fop("dve", lambda e: e.tensor_single_scalar(out=sinT[0:64, :], in_=tabS[0:64, :],
                                                 scalar=-1.0, op=ALU.mult))
    p.fop("dve", lambda e: e.tensor_copy(out=sinT[64:128, :], in_=tabS[64:128, :]))
    consts_done = p.tick("dve", "s_dve")
    if "dbg" in AB:
        p.wait("sp", "s_dve", consts_done)
        dma("sp", dbg_hbm[:, 56:64], tsc[:, :], "s_db")
        p.op("sp", lambda e: e.dma_start(out=dbg_hbm[:, 24:25], in_=iot[:, 0:1],
                                         allow_slow_non_contiguous=True),
             inc="s_db", n=16)
        dma("gp", dbg_hbm[:, 40:48], cosT[:, 0:8], "s_db")
        dma("gp", dbg_hbm[:, 48:56], sinT[:, 0:8], "s_db")
        p.wait("gp", "s_db", p.now("s_db"))

    x_loaded0 = dma("sp", x_sb[:, :, :],
                    x_hbm[0].rearrange("(k p) t -> p k t", p=128), "s_ld")
    p.op("sp", lambda e: e.dma_start(out=mu_sb[:, 0:CH],
                                     in_=mur_h.rearrange("(k p) -> p k", p=128),
                                     allow_slow_non_contiguous=True), inc="s_mu", n=16)
    mu_done = p.op("sp", lambda e: e.dma_start(
        out=mu_sb[:, CH:2 * CH], in_=muk_h.rearrange("(k p) -> p k", p=128),
        allow_slow_non_contiguous=True), inc="s_mu", n=16)
    p.wait("dve", "s_mu", mu_done)


    xbf_guard = [0]
    xsq_guard = [0]
    xn2_guard = [0]

    def settle_dve2():
        for _ in range(3):
            p.fop("dve", lambda e: e.tensor_copy(out=tsc[:, 4:8], in_=tsc[:, 0:4]))

    def settle_dve():
        for _ in range(3):
            p.op("dve", lambda e: e.tensor_copy(out=xsq_t[:, 1, 0:64],
                                                in_=xsq_t[:, 0, 0:64]))

    def ln_stats(t0, xdep, k):
        p.wait("gp", "s_pe", xbf_guard[0])
        if xdep:
            p.wait("gp", xdep[0], xdep[1])
        ca = dma("gp", xbf_t[:, :, :], x_sb[:, :, t0:t0 + c.TT], "s_xc")
        p.wait("dve", "s_xc", ca)
        p.wait("dve", "s_rb", p.now("s_rb"))
        p.wait("dve", "s_pe", xsq_guard[0])
        p.op("dve", lambda e: e.tensor_mul(out=xsq_t[:, :, :], in0=xbf_t[:, :, :],
                                           in1=xbf_t[:, :, :]))
        sqd = p.tick("dve", "s_dve")
        pe_acquire(0)
        pe_acquire(1)
        p.wait("pe", "s_xc", ca)
        p.wait("pe", "s_dve", sqd)
        for ck in range(CH):
            p.op("pe", lambda e, ck=ck: e.matmul(
                out=pb[0][:, 0:c.TT], lhsT=ones_bf[:, :], rhs=xbf_t[:, ck, :],
                start=(ck == 0), stop=(ck == CH - 1)))
        for ck in range(CH):
            p.op("pe", lambda e, ck=ck: e.matmul(
                out=pb[1][:, 0:c.TT], lhsT=ones_bf[:, :], rhs=xsq_t[:, ck, :],
                start=(ck == 0), stop=(ck == CH - 1)))
        std = p.tick("pe", "s_pe")
        xbf_guard[0] = std
        xsq_guard[0] = std
        p.wait("dve", "s_pe", std)
        settle_dve()
        sm = stash[:, k * c.TT:(k + 1) * c.TT]
        sv = stash[:, (2 + k) * c.TT:(3 + k) * c.TT]
        p.op("dve", lambda e, sm=sm: e.tensor_scalar(
            out=sm, in0=pb[0][:, 0:c.TT], scalar1=1.0 / D,
            scalar2=None, op0=ALU.mult))
        p.op("dve", lambda e: e.tensor_scalar(out=rsc[:, 0:c.TT],
                                              in0=pb[1][:, 0:c.TT],
                                              scalar1=1.0 / D, scalar2=float(EPS),
                                              op0=ALU.mult, op1=ALU.add))
        bce = p.tick("dve", "s_dve")
        bank_release(0, "s_dve", bce)
        bank_release(1, "s_dve", bce)
        p.op("dve", lambda e, sm=sm: e.scalar_tensor_tensor(
            out=a128[:, 0:c.TT], in0=sm, scalar=-1.0, in1=sm,
            op0=ALU.mult, op1=ALU.mult))
        p.op("dve", lambda e, sv=sv: e.tensor_add(out=sv, in0=rsc[:, 0:c.TT],
                                                  in1=a128[:, 0:c.TT]))

    def ln_pair(t0, xdep):
        ln_stats(t0, xdep, 0)
        ln_stats(t0 + c.TT, xdep, 1)
        W = 2 * c.TT
        svp = stash[:, 2 * c.TT:4 * c.TT]
        smp = stash[:, 0:2 * c.TT]
        p.fop("dve", lambda e, svp=svp: e.tensor_copy(
            out=mgk.bitcast(F32)[:, 0:W], in_=svp))
        p.fop("dve", lambda e: e.tensor_single_scalar(
            out=mgk[:, 0:W], in_=mgk[:, 0:W], scalar=1,
            op=ALU.arith_shift_right))
        p.fop("dve", lambda e: e.tensor_scalar(
            out=mgk[:, 0:W], in0=mgk[:, 0:W], scalar1=-1, scalar2=0x5f3759df,
            op0=ALU.mult, op1=ALU.add))
        p.fop("dve", lambda e: e.tensor_copy(out=a128[:, 0:W],
                                             in_=mgk.bitcast(F32)[:, 0:W]))
        for _it in range(3):
            p.op("dve", lambda e: e.tensor_mul(out=b128[:, 0:W], in0=a128[:, 0:W],
                                               in1=a128[:, 0:W]))
            p.op("dve", lambda e, svp=svp: e.tensor_mul(out=b128[:, 0:W],
                                                        in0=b128[:, 0:W], in1=svp))
            p.op("dve", lambda e: e.tensor_scalar(out=b128[:, 0:W],
                                                  in0=b128[:, 0:W], scalar1=-0.5,
                                                  scalar2=1.5, op0=ALU.mult,
                                                  op1=ALU.add))
            p.op("dve", lambda e: e.tensor_mul(out=a128[:, 0:W], in0=a128[:, 0:W],
                                               in1=b128[:, 0:W]))
        p.op("dve", lambda e, smp=smp: e.scalar_tensor_tensor(
            out=b128[:, 0:W], in0=smp, scalar=-1.0, in1=a128[:, 0:W],
            op0=ALU.mult, op1=ALU.mult))
        ab_done = p.tick("dve", "s_dve")
        if "norb" not in AB:
            p.wait("sp", "s_dve", ab_done)
            dma("sp", rowbuf[0:1, t0:t0 + W], a128[0:1, 0:W], "s_rb")
            dma("sp", rowbuf[32:33, t0:t0 + W], b128[0:1, 0:W], "s_rb")
        return ab_done

    def bcast_ab(rows_dep, width, from_rowbuf=None):
        t0 = from_rowbuf
        p.wait("pe", "s_dve", rows_dep)
        p.wait("pe", "s_rb", p.now("s_rb"))
        pe_acquire(1)
        p.op("pe", lambda e, t0=t0, w=width: e.matmul(
            out=pb[1][:, 0:w], lhsT=sel0[:, :], rhs=rowbuf[:, t0:t0 + w],
            start=True, stop=True))
        bc1 = p.tick("pe", "s_pe")
        p.wait("dve", "s_pe", bc1)
        p.wait("dve", "s_rb", p.now("s_rb"))
        settle_dve2()
        p.op("dve", lambda e, w=width: e.tensor_copy(out=a128[:, 0:w], in_=pb[1][:, 0:w]))
        ev1 = p.tick("dve", "s_dve")
        bank_release(1, "s_dve", ev1)
        pe_acquire(1)
        p.op("pe", lambda e, t0=t0, w=width: e.matmul(
            out=pb[1][:, 0:w], lhsT=sel32[:, :], rhs=rowbuf[:, t0:t0 + w],
            start=True, stop=True))
        bc2 = p.tick("pe", "s_pe")
        p.wait("dve", "s_pe", bc2)
        settle_dve2()
        p.op("dve", lambda e, w=width: e.tensor_copy(out=b128[:, 0:w], in_=pb[1][:, 0:w]))
        ev2 = p.tick("dve", "s_dve")
        bank_release(1, "s_dve", ev2)
        return ev2

    def apply_ln(dst, t0, width, bc_dep, a_off=0):
        p.wait("dve", "s_dve", bc_dep)
        if "apply" in AB:
            return p.tick("dve", "s_dve")
        for ck in range(CH):
            p.op("dve", lambda e, ck=ck, t0=t0, w=width, o=a_off: e.tensor_mul(
                out=dst[:, ck, 0:w], in0=x_sb[:, ck, t0:t0 + w], in1=a128[:, o:o + w]))
        for ck in range(CH):
            p.op("dve", lambda e, ck=ck, w=width, o=a_off: e.tensor_add(
                out=dst[:, ck, 0:w], in0=dst[:, ck, 0:w], in1=b128[:, o:o + w]))
        return p.tick("dve", "s_dve")

    def ffn_sweep(slot, wdep, src, post):
        p.wait("pe", wdep[0], wdep[1])
        last = 0
        for dk in range(CH):
            for sub in range(c.nST):
                bi = 2 + ((dk * c.nST + sub) % 4)
                pe_acquire(bi)
                for ck in range(CH):
                    p.op("pe", lambda e, dk=dk, ck=ck, sub=sub, bi=bi, slot=slot, src=src:
                         e.matmul(out=pb[bi][:, 0:c.ST],
                                  lhsT=wsl[slot][:, ck, dk, :],
                                  rhs=src[:, ck, sub * c.ST:(sub + 1) * c.ST],
                                  start=(ck == 0), stop=(ck == CH - 1)))
                last = p.tick("pe", "s_pe")
                post(dk, sub, bi, last)
        return last

    ffn_half0_done = [0]

    def ffn_phase(seq, sched, xdep, lim=None, tail_work=None):
        lim = lim or stage
        for half in range(2):
            if half == 1:
                ffn_half0_done[0] = p.now("s_dve")
            h0 = half * c.HB
            if half == 0:
                p.wait("dve", xdep[0], xdep[1])
                p.op("dve", lambda e: e.memset(ncar[:, :, :], 0.0))
                p.tick("dve", "s_dve")
            ntt = c.HB // c.TT
            assert ntt % 2 == 0
            for ti in range(ntt):
                t0 = h0 + ti * c.TT
                if ti % 2 == 0:
                    pair_bc = ln_pair(t0, xdep)
                bc = pair_bc
                if "dbgs" in AB and t0 == 0:
                    p.wait("sp", "s_dve", bc)
                    dma("sp", dbg_hbm[:, 0:16], rsc[:, 64:80], "s_db")
                    dma("sp", dbg_hbm[:, 16:32], rsc[:, c.TT + 64:c.TT + 80], "s_db")
                    dma("gp", dbg_hbm[:, 32:48], a128[:, 64:80], "s_db")
                    dma("gp", dbg_hbm[:, 48:64], b128[:, 64:80], "s_db")
                    p.wait("dve", "s_db", p.now("s_db"))
                apply_ln(xn_t, t0, c.TT, bc, a_off=(ti % 2) * c.TT)
                if "mix" in AB:
                    continue
                p.wait("dve", "s_pe", xsq_guard[0])
                for ck in range(CH):
                    p.op("dve", lambda e, ck=ck: e.tensor_sub(
                        out=xsq_t[:, ck, 1:c.TT], in0=xn_t[:, ck, 1:c.TT],
                        in1=xn_t[:, ck, 0:c.TT - 1]))
                    p.op("dve", lambda e, ck=ck: e.tensor_sub(
                        out=xsq_t[:, ck, 0:1], in0=xn_t[:, ck, 0:1], in1=ncar[:, ck, 0:1]))
                o = ti * c.TT
                for ck in range(CH):
                    p.op("dve", lambda e, ck=ck, o=o: e.scalar_tensor_tensor(
                        out=xr_sb[:, ck, o + 1:o + c.TT], in0=xsq_t[:, ck, 1:c.TT],
                        scalar=mu_sb[:, ck:ck + 1], in1=xn_t[:, ck, 0:c.TT - 1],
                        op0=ALU.mult, op1=ALU.add))
                    p.op("dve", lambda e, ck=ck, o=o: e.scalar_tensor_tensor(
                        out=xr_sb[:, ck, o:o + 1], in0=xsq_t[:, ck, 0:1],
                        scalar=mu_sb[:, ck:ck + 1], in1=ncar[:, ck, 0:1],
                        op0=ALU.mult, op1=ALU.add))
                    p.op("dve", lambda e, ck=ck, o=o: e.scalar_tensor_tensor(
                        out=xk_sb[:, ck, o + 1:o + c.TT], in0=xsq_t[:, ck, 1:c.TT],
                        scalar=mu_sb[:, CH + ck:CH + ck + 1], in1=xn_t[:, ck, 0:c.TT - 1],
                        op0=ALU.mult, op1=ALU.add))
                    p.op("dve", lambda e, ck=ck, o=o: e.scalar_tensor_tensor(
                        out=xk_sb[:, ck, o:o + 1], in0=xsq_t[:, ck, 0:1],
                        scalar=mu_sb[:, CH + ck:CH + ck + 1], in1=ncar[:, ck, 0:1],
                        op0=ALU.mult, op1=ALU.add))
                for ck in range(CH):
                    p.op("dve", lambda e, ck=ck: e.tensor_copy(
                        out=ncar[:, ck, 0:1], in_=xn_t[:, ck, c.TT - 1:c.TT]))
            mix_done = p.tick("dve", "s_dve")
            p.wait("pe", "s_dve", mix_done)
            if lim == "p1":
                continue

            def post_r(dk, sub, bi, mm):
                if "relu" in AB:
                    p.wait("act", "s_pe", mm)
                    p.op("act", lambda e, dk=dk, sub=sub, bi=bi: e.activation(
                        out=r_sb[:, dk, sub * c.ST:(sub + 1) * c.ST],
                        in_=pb[bi][:, 0:c.ST], func=AF.Relu))
                    bank_release(bi, "s_act", p.tick("act", "s_act"))
                    return
                if "sbact" in AB:
                    p.wait("dve", "s_pe", mm)
                    p.op("dve", lambda e, dk=dk, sub=sub, bi=bi: e.tensor_copy(
                        out=r_sb[:, dk, sub * c.ST:(sub + 1) * c.ST],
                        in_=pb[bi][:, 0:c.ST]))
                    dc = p.tick("dve", "s_dve")
                    bank_release(bi, "s_dve", dc)
                    p.wait("act", "s_dve", dc)
                    p.op("act", lambda e, dk=dk, sub=sub: e.activation(
                        out=r_sb[:, dk, sub * c.ST:(sub + 1) * c.ST],
                        in_=r_sb[:, dk, sub * c.ST:(sub + 1) * c.ST],
                        func=AF.Tanh, scale=0.5))
                    p.tick("act", "s_act")
                    return
                if "sig" in AB:
                    p.wait("dve", "s_pe", mm)
                    p.op("dve", lambda e, dk=dk, sub=sub, bi=bi: e.tensor_copy(
                        out=r_sb[:, dk, sub * c.ST:(sub + 1) * c.ST],
                        in_=pb[bi][:, 0:c.ST]))
                    bank_release(bi, "s_dve", p.tick("dve", "s_dve"))
                    return
                p.wait("act", "s_pe", mm)
                p.op("act", lambda e, dk=dk, sub=sub, bi=bi: e.activation(
                    out=r_sb[:, dk, sub * c.ST:(sub + 1) * c.ST], in_=pb[bi][:, 0:c.ST],
                    func=AF.Tanh, scale=0.5))
                ac = p.tick("act", "s_act")
                bank_release(bi, "s_act", ac)
                p.wait("dve", "s_act", ac)
                p.op("dve", lambda e, dk=dk, sub=sub: e.tensor_scalar(
                    out=r_sb[:, dk, sub * c.ST:(sub + 1) * c.ST],
                    in0=r_sb[:, dk, sub * c.ST:(sub + 1) * c.ST],
                    scalar1=0.5, scalar2=0.5, op0=ALU.mult, op1=ALU.add))
                p.tick("dve", "s_dve")

            def post_k(dk, sub, bi, mm):
                p.wait("act", "s_pe", mm)
                p.op("act", lambda e, dk=dk, sub=sub, bi=bi: e.activation(
                    out=ka_sb[:, dk, sub * c.ST:(sub + 1) * c.ST], in_=pb[bi][:, 0:c.ST],
                    func=AF.Relu))
                ac = p.tick("act", "s_act")
                bank_release(bi, "s_act", ac)
                p.wait("dve", "s_act", ac)
                p.op("dve", lambda e, dk=dk, sub=sub: e.tensor_mul(
                    out=ka_sb[:, dk, sub * c.ST:(sub + 1) * c.ST],
                    in0=ka_sb[:, dk, sub * c.ST:(sub + 1) * c.ST],
                    in1=ka_sb[:, dk, sub * c.ST:(sub + 1) * c.ST]))
                p.tick("dve", "s_dve")

            def post_v(dk, sub, bi, mm):
                p.wait("dve", "s_pe", mm)
                t0v = h0 + sub * c.ST
                p.op("dve", lambda e, dk=dk, sub=sub, bi=bi: e.tensor_mul(
                    out=xn_t.rearrange("p a b -> p (a b)")[:, 0:c.ST],
                    in0=r_sb[:, dk, sub * c.ST:(sub + 1) * c.ST], in1=pb[bi][:, 0:c.ST]))
                g = p.tick("dve", "s_dve")
                bank_release(bi, "s_dve", g)
                p.op("dve", lambda e, dk=dk, t0v=t0v: e.tensor_add(
                    out=x_sb[:, dk, t0v:t0v + c.ST], in0=x_sb[:, dk, t0v:t0v + c.ST],
                    in1=xn_t.rearrange("p a b -> p (a b)")[:, 0:c.ST]))
                p.tick("dve", "s_dve")

            slot, wdep = sched(3 * half + 0)
            last = ffn_sweep(slot, wdep, xr_sb, post_r)
            slot_guard[slot] = ("s_pe", last)
            if lim == "swr":
                continue
            slot, wdep = sched(3 * half + 1)
            last = ffn_sweep(slot, wdep, xk_sb, post_k)
            slot_guard[slot] = ("s_pe", last)
            if lim == "swk":
                continue
            if half == 1 and tail_work:
                tail_work()
            p.wait("pe", "s_dve", p.now("s_dve"))
            p.wait("pe", "s_act", p.now("s_act"))
            slot, wdep = sched(3 * half + 2)
            last = ffn_sweep(slot, wdep, ka_sb, post_v)
            slot_guard[slot] = ("s_pe", last)
        return ("s_dve", p.now("s_dve"))

    pre_tiles = [0]

    def attn_phase(seq, wk_slot, wk_dep, wq_slot, wq_loader, wvm_dep, wo_slot,
                   wo_loader, xdep):
        assert c.nTT % 2 == 0
        for ti in range(pre_tiles[0], c.nTT, 2):
            ln_pair(ti * c.TT, xdep)
        pre_tiles[0] = 0
        rows_all = p.now("s_dve")

        p.wait("pe", wk_dep[0], wk_dep[1])
        p.wait("pe", wvm_dep[0], wvm_dep[1])
        nhg = max(1, (H * DH) // 512)
        wvc = (H * DH) // nhg
        kvg = {0: xn2_guard[0], 1: 0}
        bc_next = bcast_ab(rows_all, c.TT, from_rowbuf=0)
        for bt in range(c.nTT):
            t0 = bt * c.TT
            buf = xn2_t if bt % 2 == 0 else xn_t
            gk = bt % 2
            bc = bc_next
            p.wait("dve", "s_pe", kvg[gk])
            xnd = apply_ln(buf, t0, c.TT, bc)
            if bt + 1 < c.nTT:
                bc_next = bcast_ab(p.now("s_dve"), c.TT,
                                   from_rowbuf=(bt + 1) * c.TT)
            p.wait("pe", "s_dve", xnd)
            for h in range(H):
                bi = 2 + (h % 2)
                pe_acquire(bi)
                for ck in range(CH):
                    p.op("pe", lambda e, h=h, ck=ck, bi=bi, buf=buf: e.matmul(
                        out=pb[bi][:, 0:c.TT], lhsT=wsl[wk_slot][:, h, ck, :],
                        rhs=buf[:, ck, 0:c.TT], start=(ck == 0), stop=(ck == CH - 1)))
                mm = p.tick("pe", "s_pe")
                p.wait("act", "s_pe", mm)
                p.op("act", lambda e, h=h, t0=t0, bi=bi: e.activation(
                    out=K_sb[:, h, t0:t0 + c.TT], in_=pb[bi][:, 0:c.TT],
                    func=AF.Copy))
                bank_release(bi, "s_act", p.tick("act", "s_act"))
            for jl in range(c.TT // c.JT):
                jj = t0 // c.JT + jl
                for hg in range(nhg):
                    bi = 4 + ((jl * nhg + hg) % 2)
                    pe_acquire(bi)
                    for ck in range(CH):
                        p.op("pe", lambda e, jl=jl, hg=hg, ck=ck, bi=bi, w=wvc,
                             buf=buf: e.matmul(
                                 out=pb[bi][:, 0:w],
                                 lhsT=buf[:, ck, jl * c.JT:(jl + 1) * c.JT],
                                 rhs=wvm_sb[:, ck, hg * w:(hg + 1) * w],
                                 start=(ck == 0), stop=(ck == CH - 1)))
                    mm = p.tick("pe", "s_pe")
                    p.wait("act", "s_pe", mm)
                    p.op("act", lambda e, jj=jj, hg=hg, bi=bi, w=wvc: e.activation(
                        out=V_sb[:, jj, hg * w:(hg + 1) * w], in_=pb[bi][:, 0:w],
                        func=AF.Copy))
                    bank_release(bi, "s_act", p.tick("act", "s_act"))
            kvg[gk] = p.now("s_pe")
        xn2_guard[0] = p.now("s_pe")
        if "bar1" in AB:
            p.barrier()
        p.wait("dve", "s_act", p.now("s_act"))
        kv_built = p.tick("dve", "s_dve")
        slot_guard[wk_slot] = ("s_pe", p.now("s_pe"))
        slot_guard[wq_slot] = ("s_pe", p.now("s_pe"))
        wq_dep = wq_loader()

        xscr = xn_t.rearrange("p a b -> p (a b)") if CH * c.TT >= T else scr
        p.wait("sp", "s_dve", kv_built)
        kd_prev = {}
        for h in range(H):
            buf = scr if h % 2 == 0 else xscr
            back = 2 if xscr is not scr else 1
            if h >= back:
                p.wait("sp", "s_dve", kd_prev[h - back])
            swa = dma("sp", buf[0:64, 0:T], K_sb[64:128, h, :], "s_swA")
            swb = dma("sp", buf[64:128, 0:T], K_sb[0:64, h, :], "s_swB")
            p.wait("dve", "s_swA", swa)
            p.wait("dve", "s_swB", swb)
            p.op("dve", lambda e, h=h: e.tensor_mul(out=K_sb[:, h, :], in0=K_sb[:, h, :],
                                                    in1=cosT[:, :]))
            p.op("dve", lambda e, buf=buf: e.tensor_mul(out=buf[:, 0:T], in0=buf[:, 0:T],
                                                        in1=sinT[:, :]))
            p.op("dve", lambda e, h=h, buf=buf: e.tensor_add(
                out=K_sb[:, h, :], in0=K_sb[:, h, :], in1=buf[:, 0:T]))
            kd_prev[h] = p.tick("dve", "s_dve")
        if "bar2" in AB:
            p.barrier()
        rope_k = p.now("s_dve")

        nsub = T // 512
        for h in range(H):
            p.wait("act", "s_dve", rope_k)
            p.op("act", lambda e, h=h: e.activation(out=scr[:, 0:T],
                                                    in_=K_sb[:, h, :],
                                                    func=AF.Square))
            ks = p.tick("act", "s_act")
            for sub in range(nsub):
                bnk = 6 + (sub % 2)
                wb = (sub % 2) * 512
                pe_acquire(bnk)
                p.wait("pe", "s_act", ks)
                p.op("pe", lambda e, sub=sub, wb=wb: e.matmul(
                    out=pbW[:, wb:wb + 512], lhsT=ones_bf[:, :],
                    rhs=scr[:, sub * 512:(sub + 1) * 512], start=True, stop=True))
                km = p.tick("pe", "s_pe")
                p.wait("dve", "s_pe", km)
                if sub == 0:
                    settle_dve2()
                p.op("dve", lambda e, sub=sub, wb=wb: e.tensor_reduce(
                    out=rsc[:, sub:sub + 1], in_=pbW[:, wb:wb + 512],
                    axis=AX.X, op=ALU.max))
                bank_release(bnk, "s_dve", p.tick("dve", "s_dve"))
            p.fence("dve", "s_dve")
            p.fop("dve", lambda e, h=h, nsub=nsub: e.tensor_reduce(
                out=bq_r[:, h:h + 1], in_=rsc[:, 0:nsub], axis=AX.X, op=ALU.max))
        kmax_done = p.now("s_dve")

        if "dbg" in AB:
            p.wait("sp", "s_dve", kmax_done)
            p.wait("gp", "s_dve", kmax_done)
            dma("sp", dbg_hbm[:, 0:H + 1], negB[:, :], "s_db")
            dma("sp", dbg_hbm[:, 16:16 + 2 * H + 4], bq_r[:, :], "s_db")
            dma("gp", dbg_hbm[:, 40:48], cosT[:, 0:8], "s_db")
            dma("gp", dbg_hbm[:, 48:56], sinT[:, 0:8], "s_db")
            dma("sp", dbg_hbm[:, 56:64], tsc[:, :], "s_db")
            dma("gp", dbg_hbm[:, 25:33], K_sb[:, 0, 100:108], "s_db")
            dma("gp", dbg_hbm[:, 33:37], V_sb[:, 0, 0:4], "s_db")
            dma("gp", dbg_hbm[:, 37:40], O_sb[:, 0, 0:3], "s_db")
            p.wait("gp", "s_db", p.now("s_db"))
        if "bar3" in AB:
            p.barrier()
        wo_dep = wo_loader()
        for b in range(c.nI):
            i0 = b * c.IT
            bc = bcast_ab(p.now("s_dve"), c.IT, from_rowbuf=i0)
            for sub in range(c.IT // c.TT):
                p.wait("dve", "s_pe", xn2_guard[0])
                xnd = apply_ln(xn2_t, i0 + sub * c.TT, c.TT, bc, a_off=sub * c.TT)
                p.wait("pe", "s_dve", xnd)
                p.wait("pe", wq_dep[0], wq_dep[1])
                for h in range(H):
                    bi = 2 + (h % 2)
                    pe_acquire(bi)
                    for ck in range(CH):
                        p.op("pe", lambda e, h=h, ck=ck, bi=bi: e.matmul(
                            out=pb[bi][:, 0:c.TT], lhsT=wsl[wq_slot][:, h, ck, :],
                            rhs=xn2_t[:, ck, 0:c.TT], start=(ck == 0),
                            stop=(ck == CH - 1)))
                    mm = p.tick("pe", "s_pe")
                    p.wait("dve", "s_pe", mm)
                    p.op("dve", lambda e, h=h, sub=sub, bi=bi: e.tensor_copy(
                        out=q_ap(h)[:, sub * c.TT:(sub + 1) * c.TT],
                        in_=pb[bi][:, 0:c.TT]))
                    bank_release(bi, "s_dve", p.tick("dve", "s_dve"))
                xn2_guard[0] = p.now("s_pe")
            q_built = p.now("s_dve")
            for g0 in range(0, H, c.HG):
                p.wait("sp", "s_dve", q_built if g0 == 0 else p.now("s_dve"))
                lanes = []
                for h in range(g0, g0 + c.HG):
                    o = (h - g0) * c.IT
                    la = f"s_sq{(h - g0) * 2}"
                    lb = f"s_sq{(h - g0) * 2 + 1}"
                    lanes.append((la, dma("sp", scr[0:64, o:o + c.IT],
                                          q_ap(h)[64:128, :], la)))
                    lanes.append((lb, dma("sp", scr[64:128, o:o + c.IT],
                                          q_ap(h)[0:64, :], lb)))
                for ln_, cv in lanes:
                    p.wait("dve", ln_, cv)
                for h in range(g0, g0 + c.HG):
                    o = (h - g0) * c.IT
                    p.op("dve", lambda e, h=h, i0=i0: e.tensor_mul(
                        out=q_ap(h)[:, :], in0=q_ap(h)[:, :], in1=cosT[:, i0:i0 + c.IT]))
                    p.op("dve", lambda e, o=o, i0=i0: e.tensor_mul(
                        out=scr[:, o:o + c.IT], in0=scr[:, o:o + c.IT],
                        in1=sinT[:, i0:i0 + c.IT]))
                    p.op("dve", lambda e, h=h, o=o: e.tensor_add(
                        out=q_ap(h)[:, :], in0=q_ap(h)[:, :], in1=scr[:, o:o + c.IT]))
                p.tick("dve", "s_dve")
            q_roped = p.now("s_dve")
            if "bar4" in AB:
                p.barrier()
            p.wait("dve", "s_act", p.now("s_act"))
            for g0 in range(0, H, c.HG):
                qg = q_lo if g0 == 0 else q_hi
                ng = c.HG if g0 == 0 else H - c.HG
                qf = qg.rearrange("p a b -> p (a b)")
                p.wait("act", "s_dve", p.now("s_dve"))
                p.op("act", lambda e, qf=qf, ng=ng: e.activation(
                    out=scr[:, 0:ng * c.IT], in_=qf[:, 0:ng * c.IT],
                    func=AF.Square))
                qs = p.tick("act", "s_act")
                for h in range(g0, g0 + ng):
                    bi = 2 + (h % 2)
                    pe_acquire(bi)
                    p.wait("pe", "s_act", qs)
                    p.op("pe", lambda e, h=h, g0=g0: e.matmul(
                        out=pb[2 + (h % 2)][:, 0:c.IT], lhsT=ones_bf[:, :],
                        rhs=scr[:, (h - g0) * c.IT:(h - g0 + 1) * c.IT],
                        start=True, stop=True))
                    qm = p.tick("pe", "s_pe")
                    p.wait("dve", "s_pe", qm)
                    if h == 0:
                        settle_dve2()
                    p.fop("dve", lambda e, h=h, bi=bi: e.tensor_reduce(
                        out=bq_r[:, H + h:H + h + 1], in_=pb[bi][:, 0:c.IT],
                        axis=AX.X, op=ALU.max))
                    bank_release(bi, "s_dve", p.now("s_dve"))
            p.fop("dve", lambda e: e.tensor_add(out=negB[:, 0:H], in0=bq_r[:, 0:H],
                                                in1=bq_r[:, H:2 * H]))
            p.fop("dve", lambda e: e.tensor_single_scalar(
                out=negB[:, 0:H], in_=negB[:, 0:H], scalar=-0.5, op=ALU.mult))
            negB_done = p.now("s_dve")
            p.wait("pe", "s_dve", q_roped)

            njj = c.NV * b + c.NV
            for h in range(H):
                if h % 2 == 0:
                    pvb, lb, bid_pv, bid_l = pb[4], pb[5], 4, 5
                else:
                    pvb, lb, bid_pv, bid_l = pbW[:, 0:512], pbW[:, 512:1024], 6, 7
                pe_acquire(bid_pv)
                pe_acquire(bid_l)

                def emit_pv(jj, pbu, dep, h=h, pvb=pvb, lb=lb, njj=njj):
                    p.wait("pe", dep[0], dep[1])
                    p.op("pe", lambda e, h=h, jj=jj, pbu=pbu, pvb=pvb, f=(jj == 0),
                         l=(jj == njj - 1): e.matmul(
                        out=pvb[:, 0:c.IT], lhsT=V_sb[:, jj, h * DH:(h + 1) * DH],
                        rhs=pbu[:, :], start=f, stop=l))
                    p.op("pe", lambda e, jj=jj, pbu=pbu, lb=lb, f=(jj == 0),
                         l=(jj == njj - 1): e.matmul(
                        out=lb[:, 0:c.IT], lhsT=ones_bf[:, :], rhs=pbu[:, :],
                        start=f, stop=l))
                    p.tick("pe", "s_pe")

                pend = []
                pbuf2x = rsc.bitcast(BF16)[:, 0:c.IT]
                for jj in range(njj):
                    v = jj - c.NV * b
                    diag = v >= 0
                    sb = (2, 3, 0, 1)[jj % 4]
                    pe_acquire(sb)
                    n0 = v * c.JT if diag else 0
                    w = c.IT - n0
                    p.op("pe", lambda e, h=h, jj=jj, sb=sb, n0=n0, w=w: e.matmul(
                        out=pb[sb][:, 0:w], lhsT=K_sb[:, h, jj * c.JT:(jj + 1) * c.JT],
                        rhs=q_ap(h)[:, n0:c.IT], start=True, stop=True))
                    mm = p.tick("pe", "s_pe")
                    pbu = pbuf[v] if diag else (pbuf2[0], pbuf2[1], pbuf2x)[jj % 3]
                    p.wait("act", "s_pe", mm)
                    p.wait("act", "s_dve", negB_done)
                    p.op("act", lambda e, h=h, sb=sb, n0=n0, w=w, pbu=pbu: e.activation(
                        out=pbu[:, n0:c.IT], in_=pb[sb][:, 0:w], func=AF.Exp,
                        bias=negB[:, h:h + 1]))
                    ex = p.tick("act", "s_act")
                    bank_release(sb, "s_act", ex)
                    if diag:
                        p.wait("dve", "s_act", ex)
                        p.op("dve", lambda e, pbu=pbu, n0=n0: e.tensor_mul(
                            out=pbu[:, n0:n0 + c.JT], in0=pbu[:, n0:n0 + c.JT],
                            in1=tri01[:, :]))
                        exd = p.tick("dve", "s_dve")
                        dep = ("s_dve", exd)
                    else:
                        dep = ("s_act", ex)
                    pend.append((jj, pbu, dep))
                    if len(pend) >= 3:
                        emit_pv(*pend.pop(0))
                for it in pend:
                    emit_pv(*it)
                pvl = p.now("s_pe")
                p.wait("dve", "s_pe", pvl)
                p.op("dve", lambda e, lb=lb: e.reciprocal(out=invl[:, :],
                                                          in_=lb[:, 0:c.IT]))
                p.op("dve", lambda e, h=h, pvb=pvb: e.tensor_mul(out=O_sb[:, h, :],
                                                                 in0=pvb[:, 0:c.IT],
                                                                 in1=invl[:, :]))
                od = p.tick("dve", "s_dve")
                bank_release(bid_pv, "s_dve", od)
                bank_release(bid_l, "s_dve", od)
            o_all = p.now("s_dve")
            if "bar5" in AB:
                p.barrier()
            p.wait("pe", wo_dep[0], wo_dep[1])
            p.wait("pe", "s_dve", o_all)
            for mk in range(CH):
                bnk = 6 + (mk % 2)
                wb = (mk % 2) * 512
                pe_acquire(bnk)
                for h in range(H):
                    p.op("pe", lambda e, mk=mk, h=h, wb=wb: e.matmul(
                        out=pbW[:, wb:wb + c.IT], lhsT=wsl[wo_slot][:, h, mk, :],
                        rhs=O_sb[:, h, :], start=(h == 0), stop=(h == H - 1)))
                mm = p.tick("pe", "s_pe")
                p.wait("dve", "s_pe", mm)
                p.op("dve", lambda e, mk=mk, i0=i0, wb=wb: e.tensor_add(
                    out=x_sb[:, mk, i0:i0 + c.IT], in0=x_sb[:, mk, i0:i0 + c.IT],
                    in1=pbW[:, wb:wb + c.IT]))
                bank_release(bnk, "s_dve", p.tick("dve", "s_dve"))
        if "bar6" in AB:
            p.barrier()
        slot_guard[wq_slot] = ("s_pe", p.now("s_pe"))
        slot_guard[wo_slot] = ("s_pe", p.now("s_pe"))
        return ("s_dve", p.now("s_dve"))

    p.wait("dve", "s_dve", consts_done)
    p.wait("pe", "s_dve", consts_done)
    store_done = 0
    store_h0 = [0]
    for s in range(c.NSEQ):
        if s > 0:
            if store_h0[0]:
                p.wait("sp", "s_st", store_h0[0])
                dma("sp", x_sb[:, :, 0:c.HB],
                    x_hbm[s].rearrange("(k p) t -> p k t", p=128)[:, :, 0:c.HB],
                    "s_ld")
                p.wait("sp", "s_st", store_done)
                xl = dma("sp", x_sb[:, :, c.HB:T],
                         x_hbm[s].rearrange("(k p) t -> p k t", p=128)[:, :, c.HB:T],
                         "s_ld")
            else:
                p.wait("sp", "s_st", store_done)
                xl = dma("sp", x_sb[:, :, :],
                         x_hbm[s].rearrange("(k p) t -> p k t", p=128), "s_ld")
        else:
            xl = x_loaded0
        xdep = ("s_ld", xl)
        p.wait("dve", "s_ld", xl)
        p.wait("pe", "s_ld", xl)

        ffn_slots = [0, 1, 0, 1, 0, 1]
        loaded = {}
        if s == 0:
            loaded[0] = load_ffn("r", 0)
            loaded[1] = load_ffn("k", 1)

        def mk_sched(loaded):
            def sched(idx):
                slot = ffn_slots[idx]
                if idx not in loaded:
                    loaded[idx] = load_ffn("rkv"[idx % 3], slot)
                if idx + 1 < 6 and (idx + 1) not in loaded:
                    loaded[idx + 1] = load_ffn("rkv"[(idx + 1) % 3], ffn_slots[idx + 1])
                return slot, loaded[idx]
            return sched

        if stage != "s0":
            if stage in ("s2", "full") or stage.startswith("f_"):
                def ln2_head():
                    for ti in range(0, c.nTT // 2, 2):
                        ln_pair(ti * c.TT, ("s_dve", ffn_half0_done[0]))
                    pre_tiles[0] = c.nTT // 2
                xdep = ffn_phase(s, mk_sched(loaded), xdep, tail_work=ln2_head)
            else:
                xdep = ffn_phase(s, mk_sched(loaded), xdep)
        if stage in ("p1", "swr", "swk"):
            pass

        if stage not in ("s0", "s1", "p1", "swr", "swk"):
            pass
        if stage in ("s2", "full") or stage.startswith("f_"):
            wk_dep = load_attn("wk", 0)
            wvm_dep = load_attn("wvm", 1)
            xdep = attn_phase(s, 0, wk_dep, 1, lambda: load_attn("wq", 1), wvm_dep, 0,
                              lambda: load_attn("wo", 0), xdep)

        if "dbgx" in AB:
            p.wait("sp", xdep[0], xdep[1])
            dma("sp", dbg_hbm[:, 0:32], x_sb[:, 0, 0:32], "s_db")
            p.wait("gp", "s_db", p.now("s_db"))
        if "dbgy" in AB:
            p.wait("gp", xdep[0], xdep[1] if True else 0)
        if stage.startswith("f_") or stage == "full":
            lim2 = stage[2:] if stage.startswith("f_") else "full"
            xdep = ffn_phase(s, mk_sched({}), xdep, lim=lim2)

        if stage.startswith("f_") or stage == "full":
            p.wait("sp", "s_dve", ffn_half0_done[0])
            store_h0[0] = dma(
                "sp", out_hbm[s].rearrange("(k p) t -> p k t", p=128)[:, :, 0:c.HB],
                x_sb[:, :, 0:c.HB], "s_st")
            p.wait("sp", xdep[0], xdep[1])
            store_done = dma(
                "sp", out_hbm[s].rearrange("(k p) t -> p k t", p=128)[:, :, c.HB:T],
                x_sb[:, :, c.HB:T], "s_st")
        else:
            p.wait("sp", xdep[0], xdep[1])
            store_done = dma("sp", out_hbm[s].rearrange("(k p) t -> p k t", p=128),
                             x_sb[:, :, :], "s_st")

    if "dbge" in AB:
        p.wait("gp", "s_st", store_done)
        dma("gp", dbg_hbm[:, 0:H + 1], negB[:, :], "s_db")
        dma("gp", dbg_hbm[:, 16:16 + c.IT // 32], invl[:, 0:c.IT:32], "s_db") if False \
            else dma("gp", dbg_hbm[:, 16:32], invl[:, 0:16], "s_db")
        dma("gp", dbg_hbm[:, 32:48], O_sb[:, 0, 0:16], "s_db")
        dma("gp", dbg_hbm[:, 48:64], O_sb[:, 1, 0:16], "s_db")
        p.wait("gp", "s_db", p.now("s_db"))
    p.wait("gp", "s_st", store_done)

    with nc.allow_low_precision(reason="f16 row temps are Newton-refined; operand precision is budgeted"), nc.Block() as block:
        @block.tensor
        def _(e):
            for fn in p.q["pe"]:
                fn(e)

        @block.vector
        def _(e):
            for fn in p.q["dve"]:
                fn(e)

        @block.scalar
        def _(e):
            for fn in p.q["act"]:
                fn(e)

        @block.gpsimd
        def _(e):
            for fn in p.q["gp"]:
                fn(e)

        @block.sync
        def _(e):
            for fn in p.q["sp"]:
                fn(e)

    return nc


def kernel(L0_attn_x, L0_attn_wq, L0_attn_wk, L0_attn_wv, L0_attn_wo,
           L0_ffn_W_r, L0_ffn_W_k, L0_ffn_W_v, L0_ffn_mu_r, L0_ffn_mu_k):
    c = Cfg()
    nc = build_nc(c)
    x = np.ascontiguousarray(np.asarray(L0_attn_x, dtype=np.float32))
    B = x.shape[0]
    ncore = 8
    spc = B // ncore
    f32 = lambda a: np.ascontiguousarray(np.asarray(a, dtype=np.float32))
    common = {
        "wq": f32(L0_attn_wq), "wk": f32(L0_attn_wk), "wv": f32(L0_attn_wv),
        "wo": f32(L0_attn_wo),
        "wrT": np.ascontiguousarray(np.asarray(L0_ffn_W_r, dtype=np.float32).T),
        "wkT": np.ascontiguousarray(np.asarray(L0_ffn_W_k, dtype=np.float32).T),
        "wvT": np.ascontiguousarray(np.asarray(L0_ffn_W_v, dtype=np.float32).T),
        "mur": f32(L0_ffn_mu_r), "muk": f32(L0_ffn_mu_k),
    }
    in_maps = []
    for i in range(ncore):
        xs = np.ascontiguousarray(np.transpose(x[i * spc:(i + 1) * spc], (0, 2, 1)))
        in_maps.append(dict(common, x=xs))
    res = run_bass_kernel_spmd(nc, in_maps, core_ids=list(range(ncore)))
    outs = [np.transpose(r["out"], (0, 2, 1)) for r in res.results]
    return np.ascontiguousarray(np.concatenate(outs, axis=0)).astype(np.float32)
